# version 37
# baseline (speedup 1.0000x reference)
import numpy as np
import concourse.bass as bass
import concourse.mybir as mybir
from concourse.bass_utils import run_bass_kernel_spmd

F32 = mybir.dt.float32
BF16 = mybir.dt.bfloat16
FP8 = mybir.dt.float8e4
AF = mybir.ActivationFunctionType
ALU = mybir.AluOpType
AX = mybir.AxisListType

B, T, D = 4, 4096, 1024
CH = 256
NBS = 256
L = 128
TOK = 2048
NG = TOK // L
FCN = D // 128
EPS = 1e-5
FP8MAX = 448.0
NSTAT = (B * T) // 128
SUP = 2
NSUP = NSTAT // SUP


class Ctr:

    def __init__(self, nc, name):
        self.h = nc.alloc_semaphore(name)
        self.n = 0

    def inc(self, instr, k=1):
        instr.then_inc(self.h, k)
        self.n += k
        return self.n

    def wait(self, eng, n=None):
        eng.wait_ge(self.h, self.n if n is None else n)


class Rot:

    def __init__(self, nslots):
        self.toks = [[] for _ in range(nslots)]

    def begin_write(self, eng, slot):
        for (ctr, val) in self.toks[slot]:
            eng.wait_ge(ctr.h, val)
        self.toks[slot] = []

    def used_by(self, slot, ctr):
        self.toks[slot].append((ctr, ctr.n))


def build_nc(dbg=()):
    nc = bass.Bass("TRN2", target_bir_lowering=False, debug=False)
    PE, V, S, G, SY = nc.tensor, nc.vector, nc.scalar, nc.gpsimd, nc.sync
    ENGS = [PE, V, S, G, SY]

    xs_d = nc.dram_tensor("xs", [128, NSTAT * 1024], F32,
                          kind="ExternalInput").ap()
    xga_d = nc.dram_tensor("xga", [128, NG * 1024], F32,
                           kind="ExternalInput").ap()
    xgb_d = nc.dram_tensor("xgb", [128, NG * 1024], F32,
                           kind="ExternalInput").ap()
    wall_d = nc.dram_tensor("wall", [128, 6144], F32, kind="ExternalInput").ap()
    wpm_d = nc.dram_tensor("wpm", [128, FCN * 1024], F32,
                           kind="ExternalInput").ap()
    fb_d = nc.dram_tensor("fb", [1, D], F32, kind="ExternalInput").ap()
    dlg_d = nc.dram_tensor("dlg", [1, 1], F32, kind="ExternalInput").ap()
    osc_d = nc.dram_tensor("osc", [1, 1], F32, kind="ExternalInput").ap()
    zc_d = nc.dram_tensor("zc", [1, 1], F32, kind="ExternalInput").ap()
    out_d = nc.dram_tensor("out", [128, NG * 1024], F32,
                           kind="ExternalOutput").ap()
    dbg_d = {}
    for name, shape in dbg:
        dbg_d[name] = nc.dram_tensor(name, list(shape), F32,
                                     kind="ExternalOutput").ap()

    def sb(name, shape, dtype=F32):
        return nc.alloc_sbuf_tensor(name, list(shape), dtype).ap()

    wall_sb = sb("wall_sb", [128, 6144])
    bas_sb = wall_sb[:, 0:2048]
    basT_sb = wall_sb[:, 2048:4096]
    qT_sb = wall_sb[:, 4096:4608]
    kT_sb = wall_sb[:, 4608:5120]
    vT_sb = wall_sb[:, 5120:5632]
    oc_sb = wall_sb[:, 5632:6144]
    fb_sb = sb("fb_sb", [1, D])
    fbb_sb = sb("fbb_sb", [128, D])
    dlg_sb = sb("dlg_sb", [1, 1])
    osc_sb = sb("osc_sb", [1, 1])
    osc_col = sb("osc_col", [128, 1])
    zc_sb = sb("zc_sb", [1, 1])
    zc_col = sb("zc_col", [128, 1])

    ident = sb("ident", [128, 128])
    iof = sb("iof", [128, 128])
    iop = sb("iop", [128, 128])
    ioc = sb("ioc", [128, 1])
    tri = sb("tri", [128, 128])
    l127_col = sb("l127_col", [128, 1])
    djc = sb("djc", [128, 1])
    sc1 = sb("sc1", [1, 1])
    sc2 = sb("sc2", [1, 1])
    lnd_col = sb("lnd_col", [128, 1])
    nlnd_col = sb("nlnd_col", [128, 1])
    dl_col = sb("dl_col", [128, 1])
    ones_row = sb("ones_row", [1, 128])
    one_col = sb("one_col", [128, 1])
    wu_col = sb("wu_col", [128, 1])
    xu_col = sb("xu_col", [128, 1])
    wsc2i_col = sb("wsc2i_col", [128, 1])
    xsc2i_col = sb("xsc2i_col", [128, 1])
    wdq4_col = sb("wdq4_col", [128, 1])
    amax2_col = sb("amax2_col", [128, 1])
    negd = sb("negd", [128, SUP])
    tmp11 = sb("tmp11", [1, 1])
    tmp12 = sb("tmp12", [1, 1])
    tmp13 = sb("tmp13", [1, 1])

    SUMW = sb("SUMW", [128, NSTAT])
    SQW = sb("SQW", [128, NSTAT])
    MAXDW = sb("MAXDW", [128, NSTAT])
    NEGMW = sb("NEGMW", [128, NSTAT])
    RSW = sb("RSW", [128, NSTAT])
    SCR1 = sb("SCR1", [128, NSTAT])
    SCR2 = sb("SCR2", [128, NSTAT])
    NMRW = sb("NMRW", [128, 2 * NG])
    cand_col = sb("cand_col", [128, 1])
    amax_col = sb("amax_col", [128, 1])
    xsc_col = sb("xsc_col", [128, 1])
    xsc2_col = sb("xsc2_col", [128, 1])
    xdq_col = sb("xdq_col", [128, 1])
    WM8 = sb("WM8", [128, FCN])
    wm_col = sb("wm_col", [128, 1])
    wamax_col = sb("wamax_col", [128, 1])
    wsc_col = sb("wsc_col", [128, 1])
    wsc2_col = sb("wsc2_col", [128, 1])
    wdq_col = sb("wdq_col", [128, 1])
    sxw_col = sb("sxw_col", [128, 1])

    xsup = sb("xsup", [128, 6 * SUP * 1024])
    xgrp = sb("xgrp", [128, 4 * 1024])
    xnt = sb("xnt", [128, 2 * 1024])
    xnp = sb("xnp", [128, 2 * 1024])
    zsb = sb("zsb", [128, 2 * 256])
    qk = sb("qk", [128, 2 * 512])
    vtmg = sb("vtmg", [128, 2 * 256])
    kdtm = sb("kdtm", [128, 2 * 256])
    dqkd = sb("dqkd", [128, 2 * 768])
    wsb = sb("wsb", [128, 2 * 128])
    stA = sb("stA", [128, 512])
    stB = sb("stB", [128, 512])
    RETR = sb("RETR", [128, 2 * TOK])
    ssb = sb("ssb", [128, 2 * D])
    tbf = sb("tbf", [128, D], BF16)
    gsb = sb("gsb", [128, 2 * D])
    f1b = sb("f1b", [128, D])
    outt = sb("outt", [128, 2 * D])
    xqg = sb("xqg", [128, 2 * 1024], FP8)
    wq = sb("wq", [128, FCN * D], FP8)
    wtrot = sb("wtrot", [128, 2 * D])

    ps = [nc.alloc_psum_tensor(f"ps{i}", [128, 512], F32).ap() for i in range(8)]

    d_xs = [Ctr(nc, f"d_xs{i}") for i in range(12)]
    d_wc = Ctr(nc, "d_wc")
    d_sm = Ctr(nc, "d_sm")
    d_wt = [Ctr(nc, "d_wt0"), Ctr(nc, "d_wt1")]
    d_xg = [Ctr(nc, f"d_xg{i}") for i in range(4)]
    d_out = [Ctr(nc, "d_out0"), Ctr(nc, "d_out1")]
    d_dbg = Ctr(nc, "d_dbg")
    c_pe = Ctr(nc, "c_pe")
    c_v = Ctr(nc, "c_v")
    c_s = Ctr(nc, "c_s")
    c_g = Ctr(nc, "c_g")
    st_done_v = Ctr(nc, "st_done_v")
    st_done_s = Ctr(nc, "st_done_s")
    barc = Ctr(nc, "barc")

    bank_tok = [[] for _ in range(8)]

    def pe_use_bank(i):
        for (ctr, val) in bank_tok[i]:
            PE.wait_ge(ctr.h, val)
        bank_tok[i] = []

    def bank_freed(i, ctr):
        bank_tok[i].append((ctr, ctr.n))

    def bcast(src_1xN, dst_128xN, n, bank=0, v_tok=None, s_tok=None):
        PE.wait_ge(c_v.h, c_v.n if v_tok is None else v_tok)
        PE.wait_ge(c_s.h, c_s.n if s_tok is None else s_tok)
        off = 0
        while off < n:
            w = min(512, n - off)
            pe_use_bank(bank)
            pev = c_pe.inc(PE.matmul(ps[bank][:, 0:w], ones_row[0:1, :],
                                     src_1xN[0:1, off:off + w],
                                     start=True, stop=True))
            c_pe.wait(V, pev)
            c_v.inc(V.tensor_copy(dst_128xN[:, off:off + w], ps[bank][:, 0:w]))
            bank_freed(bank, c_v)
            off += w
        return c_v.n

    def pmax_col(src_col, dst_col, bank=1, v_tok=None, s_tok=None):
        pe_use_bank(bank)
        PE.wait_ge(c_v.h, c_v.n if v_tok is None else v_tok)
        if s_tok is not None:
            PE.wait_ge(c_s.h, s_tok)
        ptv = c_pe.inc(PE.matmul(ps[bank][0:1, 0:128], src_col[:, 0:1],
                                 ident[:, :], start=True, stop=True,
                                 is_transpose=True))
        c_pe.wait(V, ptv)
        tv = c_v.inc(V.reduce_max(tmp11[0:1, 0:1], ps[bank][0:1, 0:128], AX.X))
        bank_freed(bank, c_v)
        return bcast(tmp11, dst_col, 1, bank=bank, v_tok=tv, s_tok=0)

    def barrier():
        for e in ENGS:
            barc.inc(e.nop())
        n = barc.n
        for e in ENGS:
            e.wait_ge(barc.h, n)

    def dbg_write(name, src_ap):
        if name in dbg_d:
            c_v.wait(G)
            c_s.wait(G)
            G.wait_ge(d_dbg.h, d_dbg.n)
            d_dbg.inc(G.dma_start(dbg_d[name][:, :], src_ap), 16)

    def dbg_flush():
        if d_dbg.n:
            V.wait_ge(d_dbg.h, d_dbg.n)
            S.wait_ge(d_dbg.h, d_dbg.n)

    d_sm.inc(G.dma_start(dlg_sb[:, :], dlg_d[:, :]), 16)
    d_sm.inc(G.dma_start(fb_sb[:, :], fb_d[:, :]), 16)
    d_sm.inc(G.dma_start(osc_sb[:, :], osc_d[:, :]), 16)
    d_sm.inc(G.dma_start(zc_sb[:, :], zc_d[:, :]), 16)
    small_loaded = d_sm.n
    d_wc.inc(S.dma_start(wall_sb[:, :], wall_d[:, :]), 16)
    w_loaded = d_wc.n
    d_xg[0].inc(S.dma_start(xgrp[:, 0:1024], xga_d[:, 0:1024]), 16)
    pre0 = (0, d_xg[0].n)
    d_xg[1].inc(S.dma_start(xgrp[:, 1024:2048], xga_d[:, 1024:2048]), 16)
    pre1 = (1, d_xg[1].n)

    c_g.inc(G.iota(iof[:, :], [[1, 128]], base=0, channel_multiplier=0,
                   allow_small_or_imprecise_dtypes=True))
    c_g.inc(G.iota(iop[:, :], [[0, 128]], base=0, channel_multiplier=1,
                   allow_small_or_imprecise_dtypes=True))
    giota = c_g.inc(G.iota(ioc[:, :], [[0, 1]], base=0, channel_multiplier=1,
                           allow_small_or_imprecise_dtypes=True))
    c_g.wait(V, giota)
    c_v.inc(V.memset(ones_row[:, :], 1.0))
    c_v.inc(V.memset(one_col[:, :], 1.0))
    c_v.inc(V.memset(tmp13[:, :], 0.0))
    fence_init = c_v.inc(V.memset(tmp12[:, :], 0.0))
    c_v.inc(V.memset(negd[:, :], -1.0 / D))
    c_v.inc(V.tensor_tensor(ident[:, :], iop[:, :], iof[:, :], ALU.is_equal))
    c_v.inc(V.tensor_tensor(tri[:, :], iop[:, :], iof[:, :], ALU.is_gt))


    rrctr = [0]

    def rr_bank():
        i = rrctr[0] % 8
        rrctr[0] += 1
        return i

    sup_tok = [dict() for _ in range(NSUP)]
    s_fence_tok = [0]

    def s_fence():
        if s_fence_tok[0] == 0:
            S.wait_ge(c_v.h, fence_init)
        else:
            S.wait_ge(c_s.h, s_fence_tok[0])
        s_fence_tok[0] = c_s.inc(S.activation(tmp13[0:1, 0:1], tmp13[0:1, 0:1],
                                              AF.Identity))
        return s_fence_tok[0]

    def sup_tile(s, k):
        slot = s % 6
        return xsup[:, (slot * SUP + k) * 1024:(slot * SUP + k + 1) * 1024]

    def st_load(s):
        slot = s % 6
        if s >= 6:
            SY.wait_ge(st_done_v.h, s - 5)
            SY.wait_ge(st_done_s.h, s - 5)
        dst = xsup[:, slot * SUP * 1024:(slot + 1) * SUP * 1024]
        d_xs[slot].inc(SY.dma_start(
            dst, xs_d[:, s * SUP * 1024:(s + 1) * SUP * 1024]), 16)
        sup_tok[s]["dma"] = d_xs[slot].n

    def st_sum(s):
        slot = s % 6
        S.wait_ge(d_xs[slot].h, sup_tok[s]["dma"])
        c_s.inc(S.activation(sup_tile(s, 0), sup_tile(s, 0), AF.Identity,
                             accum_out=SUMW[:, s * SUP: s * SUP + 1]))
        if s % 4 != 3:
            V.wait_ge(d_xs[slot].h, sup_tok[s]["dma"])
            c_v.inc(V.reduce_sum(SUMW[:, s * SUP + 1: s * SUP + 2],
                                 sup_tile(s, 1), AX.X))
            sup_tok[s]["sum_v"] = c_v.n
        else:
            c_s.inc(S.activation(sup_tile(s, 1), sup_tile(s, 1), AF.Identity,
                                 accum_out=SUMW[:, s * SUP + 1: s * SUP + 2]))
            sup_tok[s]["sum_v"] = None
        sup_tok[s]["sum_s"] = s_fence()

    def st_negm(s):
        t0 = s * SUP
        G.wait_ge(c_s.h, sup_tok[s]["sum_s"])
        if sup_tok[s]["sum_v"] is not None:
            G.wait_ge(c_v.h, sup_tok[s]["sum_v"])
        c_g.inc(G.tensor_tensor(NEGMW[:, t0:t0 + SUP], SUMW[:, t0:t0 + SUP],
                                negd[:, :], ALU.mult))
        sup_tok[s]["negm"] = c_g.n

    def st_dev(s):
        S.wait_ge(c_g.h, sup_tok[s]["negm"])
        for k in range(SUP):
            t = s * SUP + k
            tile = sup_tile(s, k)
            c_s.inc(S.activation(tile, tile, AF.Square, bias=NEGMW[:, t:t + 1],
                                 accum_out=SQW[:, t:t + 1]))
        s_fence()
        S.wait_ge(st_done_s.h, st_done_s.n)
        S.wait_ge(c_v.h, fence_init)
        st_done_s.inc(S.activation(tmp12[0:1, 0:1], tmp12[0:1, 0:1],
                                   AF.Identity))

    def st_max(s):
        V.wait_ge(st_done_s.h, s + 1)
        for k in range(SUP):
            t = s * SUP + k
            ins = V.reduce_max(MAXDW[:, t:t + 1], sup_tile(s, k), AX.X)
            if k == SUP - 1:
                st_done_v.inc(ins)
            else:
                c_v.inc(ins)

    NSTEP = NSUP + 4
    step_ctr = [0]
    ld_ptr = [0]

    def load_next():
        s = ld_ptr[0]
        if s < NSUP:
            st_load(s)
            ld_ptr[0] += 1

    def sup_step():
        k = step_ctr[0]
        if k >= NSTEP:
            return
        step_ctr[0] += 1
        if k < NSUP:
            load_next()
        if 0 <= k - 1 < NSUP:
            st_sum(k - 1)
        if 0 <= k - 2 < NSUP:
            st_negm(k - 2)
        if 0 <= k - 3 < NSUP:
            st_dev(k - 3)
        if 0 <= k - 4 < NSUP:
            st_max(k - 4)

    chain_toks = {}

    def chain_rstd(c0, c1, sup_hi=None):
        V.wait_ge(st_done_s.h, st_done_s.n if sup_hi is None else sup_hi)
        if sup_hi is None:
            V.wait_ge(c_s.h, c_s.n)
            V.drain()
        vv = c_v.inc(V.tensor_scalar(SCR2[:, c0:c1], SQW[:, c0:c1], 1.0 / D,
                                     EPS, op0=ALU.mult, op1=ALU.add))
        S.wait_ge(c_v.h, vv)
        sq = c_s.inc(S.activation(SCR2[:, c0:c1], SCR2[:, c0:c1], AF.Sqrt))
        V.wait_ge(c_s.h, sq)
        rv = c_v.inc(V.reciprocal(RSW[:, c0:c1], SCR2[:, c0:c1]))
        gtk = 0
        if c0 < 2 * NG:
            h1 = min(c1, 2 * NG)
            G.wait_ge(c_v.h, rv)
            G.wait_ge(c_g.h, c_g.n)
            gtk = c_g.inc(G.tensor_tensor(NMRW[:, c0:h1], NEGMW[:, c0:h1],
                                          RSW[:, c0:h1], ALU.mult))
        for c in range(c0, min(c1, 2 * NG)):
            chain_toks[c] = (rv, gtk)
        return rv

    for _ in range(6):
        load_next()
    first_hi = min(NSUP, 2)
    n_up = min(NSTEP, 5)
    for _ in range(n_up):
        sup_step()
    chain_rstd(0, min(NG, SUP * first_hi), sup_hi=first_hi)
    chained = [min(NG, SUP * first_hi)]

    S.wait_ge(d_sm.h, small_loaded)
    G.wait_ge(d_sm.h, small_loaded)
    PE.wait_ge(d_sm.h, small_loaded)
    se = c_s.inc(S.activation(sc1[:, :], dlg_sb[:, :], AF.Exp, scale=-1.0))
    V.wait_ge(c_s.h, se)
    a1 = c_v.inc(V.tensor_scalar_add(sc1[:, :], sc1[:, :], 1.0))
    V.wait_ge(c_v.h, a1)
    vd = c_v.inc(V.reciprocal(sc2[:, :], sc1[:, :]))
    S.wait_ge(c_v.h, vd)
    sl = c_s.inc(S.activation(sc1[:, :], sc2[:, :], AF.Ln))
    bcast(sc1, lnd_col, 1, bank=0, v_tok=vd, s_tok=sl)
    bcast(osc_sb, osc_col, 1, bank=1, v_tok=0, s_tok=0)
    blc = bcast(zc_sb, zc_col, 1, bank=2, v_tok=0, s_tok=0)
    V.wait_ge(c_v.h, blc)
    c_v.inc(V.tensor_scalar_mul(nlnd_col[:, :], lnd_col[:, :], -1.0))
    c_v.inc(V.tensor_scalar_mul(l127_col[:, :], lnd_col[:, :], float(L - 1)))
    vdl = c_v.inc(V.tensor_scalar_mul(sc2[:, :], sc1[:, :], float(L)))
    S.wait_ge(c_v.h, vdl)
    for rep in range(2):
        c_s.inc(S.activation(qk[:, rep * 128:rep * 128 + 128], iof[:, :],
                             AF.Exp, scale=nlnd_col[:, :], bias=nlnd_col[:, :]))
        c_s.inc(S.activation(qk[:, 256 + rep * 128:256 + rep * 128 + 128],
                             iof[:, :], AF.Exp, scale=nlnd_col[:, :],
                             bias=l127_col[:, :]))
        c_s.inc(S.activation(qk[:, 512 + rep * 128:512 + rep * 128 + 128],
                             iof[:, :], AF.Exp, scale=lnd_col[:, :]))
    c_s.inc(S.activation(djc[:, :], ioc[:, :], AF.Exp, scale=lnd_col[:, :]))
    sdl = c_s.inc(S.activation(sc2[:, :], sc2[:, :], AF.Exp))
    bcast(sc2, dl_col, 1, bank=0, v_tok=vdl, s_tok=sdl)
    bcast(fb_sb, fbb_sb, D, bank=1, v_tok=0, s_tok=0)
    c_v.inc(V.memset(stA[:, :], 0.0))
    decay_s_tok = c_s.n
    decay_v_tok = c_v.n
    V.wait_ge(c_s.h, decay_s_tok)


    xnt_rot = Rot(2)
    xnp_rot = Rot(2)
    zsb_rot = Rot(2)
    qk_rot = Rot(2)
    vtm_rot = Rot(2)
    kdt_rot = Rot(2)
    dqkd_rot = Rot(2)
    wsb_rot = Rot(2)
    sdb_rot = Rot(2)
    xgrp_rot = Rot(4)
    outt_rot = Rot(2)
    xqg_rot = Rot(2)
    ssb_rot = Rot(2)
    gsb_rot = Rot(2)
    tbf_rot = Rot(1)
    f1b_rot = Rot(1)

    def xn_ln(statcol, xtile, sl, on_g):
        xt = xnt[:, sl * 1024:(sl + 1) * 1024]
        vt, gt = chain_toks[statcol]
        if on_g:
            xnt_rot.begin_write(G, sl)
            G.wait_ge(c_v.h, vt)
            bcc = [128, 1024]
            a1 = c_g.inc(G.tensor_tensor(
                xt, xtile, NEGMW[:, statcol:statcol + 1].to_broadcast(bcc),
                ALU.add))
            G.wait_ge(c_g.h, a1)
            return (c_g, c_g.inc(G.tensor_tensor(
                xt, xt, RSW[:, statcol:statcol + 1].to_broadcast(bcc),
                ALU.mult)))
        xnt_rot.begin_write(S, sl)
        S.wait_ge(c_v.h, vt)
        S.wait_ge(c_g.h, gt)
        sv = c_s.inc(S.activation(xt, xtile, AF.Identity,
                                  scale=RSW[:, statcol:statcol + 1],
                                  bias=NMRW[:, statcol:statcol + 1]))
        return (c_s, sv)

    def xn_fm(sl, sv):
        xt = xnt[:, sl * 1024:(sl + 1) * 1024]
        xnp_rot.begin_write(V, sl)
        PE.wait_ge(sv[0].h, sv[1])
        dst = xnp[:, sl * 1024:(sl + 1) * 1024]
        for half in range(2):
            bank = rr_bank()
            pe_use_bank(bank)
            for q in range(4):
                fc = half * 4 + q
                pev = c_pe.inc(PE.matmul(ps[bank][:, q * 128:(q + 1) * 128],
                                         xt[:, fc * 128:(fc + 1) * 128],
                                         ident[:, :], start=(q == 0),
                                         stop=(q == 3), is_transpose=True))
            xnt_rot.used_by(sl, c_pe)
            c_pe.wait(V, pev)
            c_v.inc(V.tensor_copy(dst[:, half * 512:(half + 1) * 512],
                                  ps[bank][:, :]))
            bank_freed(bank, c_v)
        return c_v.n

    def project_z(sl, v_tok):
        src = xnp[:, sl * 1024:(sl + 1) * 1024]
        bank = rr_bank()
        pe_use_bank(bank)
        PE.wait_ge(c_v.h, v_tok)
        for bc in range(2):
            for fc in range(FCN):
                pev = c_pe.inc(PE.matmul(
                    ps[bank][:, bc * 128:(bc + 1) * 128],
                    bas_sb[:, fc * NBS + bc * 128: fc * NBS + bc * 128 + 128],
                    src[:, fc * 128:(fc + 1) * 128],
                    start=(bc == 0 and fc == 0),
                    stop=(bc == 1 and fc == FCN - 1)))
        xnp_rot.used_by(sl, c_pe)
        zsb_rot.begin_write(V, sl)
        c_pe.wait(V, pev)
        c_v.inc(V.tensor_copy(zsb[:, sl * 256:(sl + 1) * 256],
                              ps[bank][:, 0:256]))
        bank_freed(bank, c_v)
        return c_v.n

    def project_tm(sl, coefT, dst_ap, v_tok, scale_col=None):
        zg = zsb[:, sl * 256:(sl + 1) * 256]
        bank = rr_bank()
        pe_use_bank(bank)
        PE.wait_ge(c_v.h, v_tok)
        for bc in range(2):
            pev = c_pe.inc(PE.matmul(ps[bank][:, 0:256],
                                     zg[:, bc * 128:(bc + 1) * 128],
                                     coefT[:, bc * CH:(bc + 1) * CH],
                                     start=(bc == 0), stop=(bc == 1)))
        zsb_rot.used_by(sl, c_pe)
        if scale_col is not None:
            V.wait_ge(c_pe.h, pev)
            c_v.inc(V.tensor_scalar(dst_ap, ps[bank][:, 0:256], scale_col,
                                    None, op0=ALU.mult))
            bank_freed(bank, c_v)
            return c_v.n
        V.wait_ge(c_pe.h, pev)
        c_v.inc(V.tensor_copy(dst_ap, ps[bank][:, 0:256]))
        bank_freed(bank, c_v)
        return c_v.n

    grp_xgrp = (xgrp, xgrp_rot, d_xg, 4, 1024)

    def emit_grp_dma(gb, src_fn, pos):
        buf, rot, sems, nslots, stride = gb
        src_d, p = src_fn(pos)
        slot = pos % nslots
        rot.begin_write(S, slot)
        sems[slot].inc(S.dma_start(
            buf[:, slot * stride: slot * stride + 1024],
            src_d[:, p * 1024:(p + 1) * 1024]), 16)
        return slot, sems[slot].n

    orderA = list(range(NG - 1, -1, -1))

    def run_pipe(src_fn, statbase, order, stages, pumps_per_iter, extra=None,
                 gb=None, emit_order=None, statmap=None, ln_g=True,
                 dmaq_init=None):
        n_st = len(stages)
        nsl = len(order)
        if gb is None:
            gb = grp_xgrp
        buf, rot, sems, nslots, stride = gb
        toks = [dict() for _ in range(nsl)]
        dmaq = dict(dmaq_init) if dmaq_init else {}
        for j in range(min(2, nsl)):
            if j not in dmaq:
                dmaq[j] = emit_grp_dma(gb, src_fn, j)

        def emit_ln(j):
            sl = j % 2
            slot, dmaval = dmaq[j]
            lne = G if ln_g else S
            lne.wait_ge(sems[slot].h, dmaval)
            xtile = buf[:, slot * stride: slot * stride + 1024]
            sv = xn_ln(statmap(j) if statmap else statbase + order[j],
                       xtile, sl, ln_g)
            rot.used_by(slot, c_g if ln_g else c_s)
            toks[j]["xtile"] = xtile
            toks[j]["xslot"] = slot
            toks[j]["xrot"] = rot
            toks[j]["sv"] = sv

        emit_ln(0)
        for i in range(nsl + n_st - 1):
            if i + 1 < nsl:
                emit_ln(i + 1)
            if i + 2 < nsl:
                dmaq[i + 2] = emit_grp_dma(gb, src_fn, i + 2)
            for k in (emit_order if emit_order is not None else range(n_st)):
                j = i - k
                if 0 <= j < nsl:
                    stages[k](j, order[j], j % 2, toks[j])
            np_i = (pumps_per_iter(i) if callable(pumps_per_iter)
                    else pumps_per_iter)
            for _ in range(np_i):
                sup_step()
            if extra is not None and i < nsl:
                extra(i)

    st_cur, st_nxt = stA, stB
    stv = [(c_v, c_v.n)]

    def s_tr(j, g, sl, tk):
        tk["vtok"] = xn_fm(sl, tk["sv"])

    def s_z(j, g, sl, tk):
        tk["ztok"] = project_z(sl, tk["vtok"])

    def a_kv(j, g, sl, tk):
        kdt_rot.begin_write(V, sl)
        tk["ktok"] = project_tm(sl, kT_sb, kdtm[:, sl * 256:(sl + 1) * 256],
                                tk["ztok"], scale_col=djc[:, :])
        vtm_rot.begin_write(V, sl)
        tk["vtok2"] = project_tm(sl, vT_sb, vtmg[:, sl * 256:(sl + 1) * 256],
                                 tk["ztok"])

    def a_state(j, g, sl, tk):
        nonlocal st_cur, st_nxt
        bank = rr_bank()
        pe_use_bank(bank)
        PE.wait_ge(c_v.h, max(tk["ktok"], tk["vtok2"]))
        kd_ap = kdtm[:, sl * 256:(sl + 1) * 256]
        vt_ap = vtmg[:, sl * 256:(sl + 1) * 256]
        c_pe.inc(PE.matmul(ps[bank][:, 0:256], kd_ap[:, 0:128], vt_ap,
                           start=True, stop=False))
        pev = c_pe.inc(PE.matmul(ps[bank][:, 256:512], kd_ap[:, 128:256], vt_ap,
                                 start=False, stop=True))
        kdt_rot.used_by(sl, c_pe)
        vtm_rot.used_by(sl, c_pe)
        V.wait_ge(c_pe.h, pev)
        V.wait_ge(stv[0][0].h, stv[0][1])
        stv[0] = (c_v, c_v.inc(V.scalar_tensor_tensor(
            st_nxt[:, :], st_cur[:, :], dl_col[:, :], ps[bank][:, :],
            ALU.mult, ALU.add)))
        bank_freed(bank, c_v)
        st_cur, st_nxt = st_nxt, st_cur

    wrot = Rot(2)
    w_issue = [0]
    wtok = {}
    wsc2_tok = [0]

    def w_dma_issue():
        i = w_issue[0]
        if i >= 2 * FCN:
            return
        w_issue[0] += 1
        ii = i % FCN
        slot = i % 2
        wrot.begin_write(S, slot)
        d_wt[slot].inc(S.dma_start(wtrot[:, slot * D:(slot + 1) * D],
                                   wpm_d[:, ii * 1024:(ii + 1) * 1024]), 16)
        wtok[i] = d_wt[slot].n

    def wsc_chain():
        V.drain()
        wmv = c_v.inc(V.reduce_max(wm_col[:, :], WM8[:, :], AX.X))
        pt = pmax_col(wm_col, wamax_col, bank=rr_bank(), v_tok=wmv)
        V.wait_ge(c_v.h, pt)
        gu = c_v.inc(V.tensor_scalar_max(wu_col[:, :], wamax_col[:, :], 1.0))
        S.wait_ge(c_v.h, gu)
        si = c_s.inc(S.activation(wsc2i_col[:, :], wu_col[:, :], AF.Identity,
                                  scale=1.0 / 224.0))
        c_s.inc(S.activation(wdq4_col[:, :], wu_col[:, :], AF.Identity,
                             scale=4.0 / FP8MAX))
        V.wait_ge(c_s.h, c_s.n)
        wsc2_tok[0] = c_v.inc(V.reciprocal(wsc2_col[:, :], wsc2i_col[:, :]))

    def w_extra(idx):
        if idx < FCN:
            slot = idx % 2
            V.wait_ge(d_wt[slot].h, wtok[idx])
            c_v.inc(V.reduce_max(WM8[:, idx:idx + 1],
                                 wtrot[:, slot * D:(slot + 1) * D],
                                 AX.X, apply_absolute_value=True))
            wrot.used_by(slot, c_v)
            w_dma_issue()
            return
        if idx == FCN:
            wsc_chain()
        i = idx - FCN
        if i >= FCN:
            return
        slot = idx % 2
        S.wait_ge(d_wt[slot].h, wtok[idx])
        S.wait_ge(c_v.h, wsc2_tok[0])
        c_s.inc(S.activation(wq[:, i * D:(i + 1) * D],
                             wtrot[:, slot * D:(slot + 1) * D],
                             AF.Identity, scale=wsc2_col[:, :]))
        wrot.used_by(slot, c_s)
        w_dma_issue()

    pumps = min(2, -(-(NSTEP - n_up) // (2 * NG))) if NSTEP > n_up else 0


    PE.wait_ge(d_wc.h, w_loaded)

    def b_qkv(j, g, sl, tk):
        ztok = tk["ztok"]
        zg = zsb[:, sl * 256:(sl + 1) * 256]
        bank = rr_bank()
        pe_use_bank(bank)
        PE.wait_ge(c_v.h, ztok)
        first = True
        for half, coefT in ((0, qT_sb), (1, kT_sb)):
            for ck in range(2):
                for bc in range(2):
                    pev = c_pe.inc(PE.matmul(
                        ps[bank][:, half * 256 + ck * 128:
                                 half * 256 + ck * 128 + 128],
                        coefT[:, bc * CH + ck * 128: bc * CH + ck * 128 + 128],
                        zg[:, bc * 128:(bc + 1) * 128],
                        start=first, stop=(half == 1 and ck == 1 and bc == 1)))
                    first = False
        zsb_rot.used_by(sl, c_pe)
        dqkd_rot.begin_write(V, sl)
        base = sl * 768
        V.wait_ge(c_pe.h, pev)
        c_v.inc(V.tensor_tensor(dqkd[:, base:base + 256], ps[bank][:, 0:256],
                                qk[:, 0:256], ALU.mult))
        c_v.inc(V.tensor_tensor(dqkd[:, base + 256:base + 512],
                                ps[bank][:, 0:256], qk[:, 256:512], ALU.mult))
        vkd = c_v.inc(V.tensor_tensor(dqkd[:, base + 512:base + 768],
                                      ps[bank][:, 256:512], qk[:, 512:768],
                                      ALU.mult))
        bank_freed(bank, c_v)
        vtm_rot.begin_write(V, sl)
        tk["vtok2"] = project_tm(sl, vT_sb, vtmg[:, sl * 256:(sl + 1) * 256],
                                 ztok)
        tk["vkd"] = vkd

    def b_chunk(j, g, sl, tk):
        nonlocal st_cur, st_nxt
        vkd, vtok2 = tk["vkd"], tk["vtok2"]
        base = sl * 768
        dq = dqkd[:, base: base + 256]
        dq2 = dqkd[:, base + 256: base + 512]
        kd = dqkd[:, base + 512: base + 768]
        vtm_ap = vtmg[:, sl * 256:(sl + 1) * 256]
        bank_w = rr_bank()
        pe_use_bank(bank_w)
        PE.wait_ge(c_v.h, vkd)
        c_pe.inc(PE.matmul(ps[bank_w][:, 0:128], kd[:, 0:128], dq[:, 0:128],
                           start=True, stop=False))
        pw = c_pe.inc(PE.matmul(ps[bank_w][:, 0:128], kd[:, 128:256],
                                dq[:, 128:256], start=False, stop=True))
        bank_kt = rr_bank()
        pe_use_bank(bank_kt)
        c_pe.inc(PE.matmul(ps[bank_kt][:, 0:128], kd[:, 0:128], ident[:, :],
                           start=True, stop=False, is_transpose=True))
        ptr = c_pe.inc(PE.matmul(ps[bank_kt][:, 128:256], kd[:, 128:256],
                                 ident[:, :], start=False, stop=True,
                                 is_transpose=True))
        wsb_rot.begin_write(V, sl)
        V.wait_ge(c_pe.h, pw)
        wv = c_v.inc(V.tensor_tensor(wsb[:, sl * 128:(sl + 1) * 128],
                                     ps[bank_w][:, 0:128], tri[:, :], ALU.mult))
        bank_freed(bank_w, c_v)
        kdt_rot.begin_write(S, sl)
        S.wait_ge(c_pe.h, ptr)
        kdv = c_s.inc(S.activation(kdtm[:, sl * 256:(sl + 1) * 256],
                                   ps[bank_kt][:, 0:256], AF.Identity))
        bank_freed(bank_kt, c_s)
        w_ap = wsb[:, sl * 128:(sl + 1) * 128]
        bank_r = rr_bank()
        pe_use_bank(bank_r)
        PE.wait_ge(c_v.h, max(wv, vtok2))
        PE.wait_ge(stv[0][0].h, stv[0][1])
        first = True
        for cv in range(2):
            col = cv * 128
            c_pe.inc(PE.matmul(ps[bank_r][:, col:col + 128],
                               vtm_ap[:, cv * 128:(cv + 1) * 128], w_ap,
                               start=first, stop=False))
            first = False
            c_pe.inc(PE.matmul(ps[bank_r][:, col:col + 128],
                               st_cur[:, cv * 128:cv * 128 + 128], dq2[:, 0:128],
                               start=False, stop=False))
            pr = c_pe.inc(PE.matmul(ps[bank_r][:, col:col + 128],
                                    st_cur[:, 256 + cv * 128:256 + cv * 128 + 128],
                                    dq2[:, 128:256], start=False,
                                    stop=(cv == 1)))
        bank_u = rr_bank()
        pe_use_bank(bank_u)
        PE.wait_ge(c_s.h, kdv)
        kd_tm = kdtm[:, sl * 256:(sl + 1) * 256]
        c_pe.inc(PE.matmul(ps[bank_u][:, 0:256], kd_tm[:, 0:128], vtm_ap,
                           start=True, stop=False))
        pst = c_pe.inc(PE.matmul(ps[bank_u][:, 256:512], kd_tm[:, 128:256],
                                 vtm_ap, start=False, stop=True))
        kdt_rot.used_by(sl, c_pe)
        vtm_rot.used_by(sl, c_pe)
        dqkd_rot.used_by(sl, c_pe)
        wsb_rot.used_by(sl, c_pe)
        S.wait_ge(c_pe.h, pr)
        c_s.inc(S.activation(RETR[:, g * 128:(g + 1) * 128],
                             ps[bank_r][:, 0:128], AF.Identity))
        tk["retr"] = c_s.inc(S.activation(RETR[:, TOK + g * 128:
                                               TOK + (g + 1) * 128],
                                          ps[bank_r][:, 128:256], AF.Identity))
        bank_freed(bank_r, c_s)
        V.wait_ge(c_pe.h, pst)
        V.wait_ge(stv[0][0].h, stv[0][1])
        stv[0] = (c_v, c_v.inc(V.scalar_tensor_tensor(
            st_nxt[:, :], st_cur[:, :], dl_col[:, :], ps[bank_u][:, :],
            ALU.mult, ALU.add)))
        bank_freed(bank_u, c_v)
        st_cur, st_nxt = st_nxt, st_cur

    def c_r2(j, g, sl, tk):
        bank = rr_bank()
        pe_use_bank(bank)
        PE.wait_ge(c_s.h, tk["retr"])
        first = True
        for bc2 in range(2):
            for cc in range(2):
                p2 = c_pe.inc(PE.matmul(
                    ps[bank][:, bc2 * 128:(bc2 + 1) * 128],
                    oc_sb[:, cc * NBS + bc2 * 128: cc * NBS + bc2 * 128 + 128],
                    RETR[:, cc * TOK + g * 128: cc * TOK + (g + 1) * 128],
                    start=first, stop=(bc2 == 1 and cc == 1)))
                first = False
        V.wait_ge(c_pe.h, p2)
        c_v.inc(V.tensor_copy(RETR[:, g * 128:(g + 1) * 128],
                              ps[bank][:, 0:128]))
        c_v.inc(V.tensor_copy(RETR[:, TOK + g * 128: TOK + (g + 1) * 128],
                              ps[bank][:, 128:256]))
        bank_freed(bank, c_v)

    def u_proj(j, g, sl, tk):
        (a_kv if j < NG else b_qkv)(j, g, sl, tk)

    def u_mix(j, g, sl, tk):
        if j < NG:
            a_state(j, g, sl, tk)
            return
        if j == NG:
            ctr0, v0 = stv[0]
            V.wait_ge(ctr0.h, v0)
            stv[0] = (c_v, c_v.inc(V.tensor_scalar(
                st_cur[:, :], st_cur[:, :], zc_col[:, :], None, op0=ALU.mult)))
            dbg_write("dbg_stateA", st_cur[:, :])
            dbg_flush()
        b_chunk(j, g, sl, tk)

    def u_r2(j, g, sl, tk):
        if j >= NG:
            c_r2(j, g, sl, tk)

    def ab_extra(i):
        if chained[0] < 2 * NG:
            c0 = chained[0]
            chain_rstd(c0, c0 + 2, sup_hi=min(NSUP, (c0 + 2) // 2))
            chained[0] += 2
        if i == NG - 2:
            w_dma_issue()
            w_dma_issue()
        if i >= NG:
            w_extra(i - NG)

    run_pipe(lambda pos: (xga_d, pos) if pos < NG else (xgb_d, pos - NG),
             0, orderA + orderA, [s_tr, s_z, u_proj, u_mix, u_r2],
             lambda i: 2 if i < 3 else (1 if i < 12 else 2),
             extra=ab_extra, statmap=lambda j: j,
             dmaq_init={0: pre0, 1: pre1})
    for idx in range(NG, 2 * FCN):
        w_extra(idx)
    dbg_write("dbg_stateB", st_cur[:, :])
    dbg_flush()

    while step_ctr[0] < NSTEP:
        sup_step()
    rv_all = chain_rstd(0, NSTAT)
    S.wait_ge(c_v.h, rv_all)
    S.wait_ge(c_s.h, c_s.n)
    sq2 = c_s.inc(S.activation(SCR2[:, :], RSW[:, :], AF.Square))
    V.wait_ge(c_s.h, sq2)
    c_v.inc(V.tensor_mul(SCR1[:, :], MAXDW[:, :], SCR2[:, :]))
    vd = c_v.inc(V.drain())
    c_v.wait(V, vd)
    rv = c_v.inc(V.reduce_max(cand_col[:, :], SCR1[:, :], AX.X))
    pmax_col(cand_col, amax2_col, bank=rr_bank(), v_tok=rv)
    S.wait_ge(c_v.h, c_v.n)
    sqa = c_s.inc(S.activation(amax_col[:, :], amax2_col[:, :], AF.Sqrt))
    V.wait_ge(c_s.h, sqa)
    gu = c_v.inc(V.tensor_scalar_max(xu_col[:, :], amax_col[:, :], 1.0))
    S.wait_ge(c_v.h, gu)
    c_s.inc(S.activation(xsc2i_col[:, :], xu_col[:, :], AF.Identity,
                         scale=1.0 / 224.0))
    sxd = c_s.inc(S.activation(xdq_col[:, :], xu_col[:, :], AF.Identity,
                               scale=1.0 / FP8MAX))
    S.wait_ge(c_s.h, sxd)
    sxw_tok = c_s.inc(S.activation(sxw_col[:, :], xdq_col[:, :], AF.Identity,
                                   scale=wdq4_col[:, :]))
    V.wait_ge(c_s.h, sxw_tok)
    c_v.inc(V.reciprocal(xsc2_col[:, :], xsc2i_col[:, :]))
    dbg_write("dbg_amax", amax_col[:, :])
    dbg_write("dbg_rs", RSW[:, :])
    dbg_write("dbg_negm", NEGMW[:, :])
    dbg_write("dbg_wq", wq[:, 0:D])
    dbg_flush()

    barrier()

    def c_quant(j, g, sl, tk):
        xqg_rot.begin_write(V, sl)
        src = xnp[:, sl * 1024:(sl + 1) * 1024]
        V.wait_ge(c_v.h, tk["vtok"])
        tk["qv"] = c_v.inc(V.tensor_scalar(xqg[:, sl * 1024:(sl + 1) * 1024],
                                           src, xsc2_col[:, :], None,
                                           op0=ALU.mult))
        xnp_rot.used_by(sl, c_v)

    def c_sffn(j, g, sl, tk):
        qv = tk["qv"]
        ssb_rot.begin_write(S, sl)
        for fh in range(2):
            bank = rr_bank()
            pe_use_bank(bank)
            for bc in range(2):
                pp = c_pe.inc(PE.matmul(
                    ps[bank][:, 0:512],
                    RETR[:, bc * TOK + g * 128: bc * TOK + (g + 1) * 128],
                    basT_sb[:, bc * D + fh * 512: bc * D + fh * 512 + 512],
                    start=(bc == 0), stop=(bc == 1)))
            S.wait_ge(c_pe.h, pp)
            c_s.inc(S.activation(ssb[:, sl * D + fh * 512: sl * D + fh * 512 + 512],
                                 ps[bank][:, :], AF.Identity,
                                 scale=osc_col[:, :]))
            bank_freed(bank, c_s)
        xq_ap = xqg[:, sl * 1024:(sl + 1) * 1024]
        gsb_rot.begin_write(S, sl)
        PE.wait_ge(c_v.h, qv)
        for oh in range(2):
            bank = rr_bank()
            pe_use_bank(bank)
            for fc in range(FCN):
                pf = c_pe.inc(PE.matmul(
                    ps[bank][:, 0:512], xq_ap[:, fc * 128:(fc + 1) * 128],
                    wq[:, fc * D + oh * 512: fc * D + oh * 512 + 512],
                    start=(fc == 0), stop=(fc == FCN - 1)))
            xqg_rot.used_by(sl, c_pe)
            if oh == 0:
                tbf_rot.begin_write(V, 0)
            V.wait_ge(c_pe.h, pf)
            tv = c_v.inc(V.scalar_tensor_tensor(
                tbf[:, oh * 512:(oh + 1) * 512], ps[bank][:, :], sxw_col[:, :],
                fbb_sb[:, oh * 512:(oh + 1) * 512], ALU.mult, ALU.add))
            bank_freed(bank, c_v)
            S.wait_ge(c_v.h, tv)
            stok = c_s.inc(S.activation(gsb[:, sl * D + oh * 512:
                                            sl * D + oh * 512 + 512],
                                        tbf[:, oh * 512:(oh + 1) * 512],
                                        AF.Silu))
        tbf_rot.used_by(0, c_s)
        tk["gtok"] = stok

    def c_comb(j, g, sl, tk):
        f1b_rot.begin_write(G, 0)
        G.wait_ge(c_s.h, tk["gtok"])
        gmt = c_g.inc(G.tensor_mul(f1b[:, :], ssb[:, sl * D:(sl + 1) * D],
                                   gsb[:, sl * D:(sl + 1) * D]))
        ssb_rot.used_by(sl, c_g)
        gsb_rot.used_by(sl, c_g)
        oslot = sl
        slot = tk["xslot"]
        outt_rot.begin_write(V, oslot)
        V.wait_ge(c_g.h, gmt)
        ov = c_v.inc(V.tensor_add(outt[:, oslot * D:(oslot + 1) * D],
                                  f1b[:, :], tk["xtile"]))
        f1b_rot.used_by(0, c_v)
        tk["xrot"].used_by(slot, c_v)
        SY.wait_ge(c_v.h, ov)
        d_out[oslot].inc(SY.dma_start(out_d[:, g * 1024:(g + 1) * 1024],
                                      outt[:, oslot * D:(oslot + 1) * D]), 16)
        outt_rot.used_by(oslot, d_out[oslot])

    xsup_b2_rot = Rot(12)
    run_pipe(lambda pos: (xgb_d, pos), NG, orderA,
             [s_tr, c_quant, c_sffn, c_comb], 0,
             gb=(xsup, xsup_b2_rot, d_xs, 12, 1024), emit_order=[0, 2, 3, 1],
             statmap=lambda j: NG + j, ln_g=False)

    if d_dbg.n:
        G.wait_ge(d_dbg.h, d_dbg.n)
    SY.wait_ge(d_out[0].h, d_out[0].n)
    SY.wait_ge(d_out[1].h, d_out[1].n)
    return nc


def kernel(**inputs):
    xin = np.ascontiguousarray(np.asarray(inputs["L0_attn_x"], dtype=np.float32))
    x2d = xin.reshape(B * T, D)
    basis = np.ascontiguousarray(np.asarray(inputs["L0_attn_basis"], np.float32))
    qc = np.asarray(inputs["L0_attn_q_coeffs"], np.float32)
    kc = np.asarray(inputs["L0_attn_k_coeffs"], np.float32)
    vc = np.asarray(inputs["L0_attn_v_coeffs"], np.float32)
    ocf = np.ascontiguousarray(np.asarray(inputs["L0_attn_o_coeffs"], np.float32))
    dlg = np.asarray(inputs["L0_attn_decay_logit"], np.float32).reshape(1, 1)
    osc = np.asarray(inputs["L0_attn_out_scale"], np.float32).reshape(1, 1)
    w = np.asarray(inputs["L0_ffn_weight"], np.float32)
    fb = np.asarray(inputs["L0_ffn_bias"], np.float32).reshape(1, D)

    basT = np.ascontiguousarray(basis.T)
    qT = np.ascontiguousarray(qc.T)
    kT = np.ascontiguousarray(kc.T)
    vT = np.ascontiguousarray(vc.T)
    wT = np.ascontiguousarray(w.T)

    def pmaj(rows):
        n = rows.shape[0] // 128
        return np.ascontiguousarray(
            rows.reshape(n, 128, D).transpose(1, 0, 2).reshape(128, n * D))

    def pmaj_desc(rows):
        n = rows.shape[0] // 128
        return np.ascontiguousarray(
            rows.reshape(n, 128, D)[::-1].transpose(1, 0, 2).reshape(128, n * D))

    wpm = pmaj(wT)

    def blk(mat, n):
        r = np.asarray(mat, np.float32).reshape(n, 128, -1)
        return r.transpose(1, 0, 2).reshape(128, -1)

    wall = np.ascontiguousarray(np.concatenate(
        [blk(basis, 8), blk(basT, 2), blk(qT, 2), blk(kT, 2), blk(vT, 2),
         blk(ocf, 2)], axis=1))
    nc = build_nc()
    in_maps = []
    for core in range(8):
        b, h = divmod(core, 2)
        own0 = b * T + (2048 if h == 0 else 0)
        own = x2d[own0:own0 + TOK]
        others = np.concatenate([x2d[:own0], x2d[own0 + TOK:]], axis=0)
        pr0 = b * T if h == 1 else 0
        prev = others[pr0:pr0 + TOK]
        rest = np.concatenate([others[:pr0], others[pr0 + TOK:]], axis=0)
        prevd = prev.reshape(NG, 128, D)[::-1].reshape(TOK, D)
        ownd = own.reshape(NG, 128, D)[::-1].reshape(TOK, D)
        in_maps.append({
            "xs": pmaj(np.concatenate([prevd, ownd, rest], axis=0)),
            "xga": pmaj_desc(prev),
            "xgb": pmaj_desc(own),
            "wall": wall, "wpm": wpm, "fb": fb, "dlg": dlg, "osc": osc,
            "zc": np.array([[1.0 if h == 1 else 0.0]], np.float32),
        })
    res = run_bass_kernel_spmd(nc, in_maps, core_ids=list(range(8)))
    y = np.empty((B, T, D), np.float32)
    for core in range(8):
        b, h = divmod(core, 2)
        t0 = 2048 if h == 0 else 0
        opm = np.asarray(res.results[core]["out"])
        y[b, t0:t0 + TOK] = (opm.reshape(128, NG, D).transpose(1, 0, 2)
                             .reshape(TOK, D))
    return y


# version 38
# speedup vs baseline: 1.0349x; 1.0349x over previous
import numpy as np
import concourse.bass as bass
import concourse.mybir as mybir
from concourse.bass_utils import run_bass_kernel_spmd

F32 = mybir.dt.float32
BF16 = mybir.dt.bfloat16
FP8 = mybir.dt.float8e4
AF = mybir.ActivationFunctionType
ALU = mybir.AluOpType
AX = mybir.AxisListType

B, T, D = 4, 4096, 1024
CH = 256
NBS = 256
L = 128
TOK = 2048
NG = TOK // L
FCN = D // 128
EPS = 1e-5
FP8MAX = 448.0
NSTAT = (B * T) // 128
SUP = 2
NSUP = NSTAT // SUP


class Ctr:

    def __init__(self, nc, name):
        self.h = nc.alloc_semaphore(name)
        self.n = 0

    def inc(self, instr, k=1):
        instr.then_inc(self.h, k)
        self.n += k
        return self.n

    def wait(self, eng, n=None):
        eng.wait_ge(self.h, self.n if n is None else n)


class Rot:

    def __init__(self, nslots):
        self.toks = [[] for _ in range(nslots)]

    def begin_write(self, eng, slot):
        for (ctr, val) in self.toks[slot]:
            eng.wait_ge(ctr.h, val)
        self.toks[slot] = []

    def used_by(self, slot, ctr):
        self.toks[slot].append((ctr, ctr.n))


def build_nc(dbg=()):
    nc = bass.Bass("TRN2", target_bir_lowering=False, debug=False)
    PE, V, S, G, SY = nc.tensor, nc.vector, nc.scalar, nc.gpsimd, nc.sync
    ENGS = [PE, V, S, G, SY]

    xs_d = nc.dram_tensor("xs", [128, NSTAT * 1024], F32,
                          kind="ExternalInput").ap()
    xga_d = nc.dram_tensor("xga", [128, NG * 1024], F32,
                           kind="ExternalInput").ap()
    xgb_d = nc.dram_tensor("xgb", [128, NG * 1024], F32,
                           kind="ExternalInput").ap()
    wall_d = nc.dram_tensor("wall", [128, 6144], F32, kind="ExternalInput").ap()
    wpm_d = nc.dram_tensor("wpm", [128, FCN * 1024], F32,
                           kind="ExternalInput").ap()
    fb_d = nc.dram_tensor("fb", [1, D], F32, kind="ExternalInput").ap()
    dlg_d = nc.dram_tensor("dlg", [1, 1], F32, kind="ExternalInput").ap()
    osc_d = nc.dram_tensor("osc", [1, 1], F32, kind="ExternalInput").ap()
    zc_d = nc.dram_tensor("zc", [1, 1], F32, kind="ExternalInput").ap()
    out_d = nc.dram_tensor("out", [128, NG * 1024], F32,
                           kind="ExternalOutput").ap()
    dbg_d = {}
    for name, shape in dbg:
        dbg_d[name] = nc.dram_tensor(name, list(shape), F32,
                                     kind="ExternalOutput").ap()

    def sb(name, shape, dtype=F32):
        return nc.alloc_sbuf_tensor(name, list(shape), dtype).ap()

    wall_sb = sb("wall_sb", [128, 6144])
    bas_sb = wall_sb[:, 0:2048]
    basT_sb = wall_sb[:, 2048:4096]
    qT_sb = wall_sb[:, 4096:4608]
    kT_sb = wall_sb[:, 4608:5120]
    vT_sb = wall_sb[:, 5120:5632]
    oc_sb = wall_sb[:, 5632:6144]
    fb_sb = sb("fb_sb", [1, D])
    fbb_sb = sb("fbb_sb", [128, D])
    dlg_sb = sb("dlg_sb", [1, 1])
    osc_sb = sb("osc_sb", [1, 1])
    osc_col = sb("osc_col", [128, 1])
    zc_sb = sb("zc_sb", [1, 1])
    zc_col = sb("zc_col", [128, 1])

    ident = sb("ident", [128, 128])
    iof = sb("iof", [128, 128])
    iop = sb("iop", [128, 128])
    ioc = sb("ioc", [128, 1])
    tri = sb("tri", [128, 128])
    l127_col = sb("l127_col", [128, 1])
    djc = sb("djc", [128, 1])
    sc1 = sb("sc1", [1, 1])
    sc2 = sb("sc2", [1, 1])
    lnd_col = sb("lnd_col", [128, 1])
    nlnd_col = sb("nlnd_col", [128, 1])
    dl_col = sb("dl_col", [128, 1])
    ones_row = sb("ones_row", [1, 128])
    one_col = sb("one_col", [128, 1])
    wu_col = sb("wu_col", [128, 1])
    xu_col = sb("xu_col", [128, 1])
    wsc2i_col = sb("wsc2i_col", [128, 1])
    xsc2i_col = sb("xsc2i_col", [128, 1])
    wdq4_col = sb("wdq4_col", [128, 1])
    amax2_col = sb("amax2_col", [128, 1])
    negd = sb("negd", [128, SUP])
    tmp11 = sb("tmp11", [1, 1])
    tmp12 = sb("tmp12", [1, 1])
    tmp13 = sb("tmp13", [1, 1])

    SUMW = sb("SUMW", [128, NSTAT])
    SQW = sb("SQW", [128, NSTAT])
    MAXDW = sb("MAXDW", [128, NSTAT])
    NEGMW = sb("NEGMW", [128, NSTAT])
    RSW = sb("RSW", [128, NSTAT])
    SCR1 = sb("SCR1", [128, NSTAT])
    SCR2 = sb("SCR2", [128, NSTAT])
    NMRW = sb("NMRW", [128, 2 * NG])
    cand_col = sb("cand_col", [128, 1])
    amax_col = sb("amax_col", [128, 1])
    xsc_col = sb("xsc_col", [128, 1])
    xsc2_col = sb("xsc2_col", [128, 1])
    xdq_col = sb("xdq_col", [128, 1])
    WM8 = sb("WM8", [128, FCN])
    wm_col = sb("wm_col", [128, 1])
    wamax_col = sb("wamax_col", [128, 1])
    wsc_col = sb("wsc_col", [128, 1])
    wsc2_col = sb("wsc2_col", [128, 1])
    wdq_col = sb("wdq_col", [128, 1])
    sxw_col = sb("sxw_col", [128, 1])

    xsup = sb("xsup", [128, 6 * SUP * 1024])
    xgrp = sb("xgrp", [128, 4 * 1024])
    xnt = sb("xnt", [128, 2 * 1024])
    xnp = sb("xnp", [128, 2 * 1024])
    zsb = sb("zsb", [128, 2 * 256])
    qk = sb("qk", [128, 2 * 512])
    vtmg = sb("vtmg", [128, 2 * 256])
    kdtm = sb("kdtm", [128, 2 * 256])
    dqkd = sb("dqkd", [128, 2 * 768])
    wsb = sb("wsb", [128, 2 * 128])
    stA = sb("stA", [128, 512])
    stB = sb("stB", [128, 512])
    RETR = sb("RETR", [128, 2 * TOK])
    ssb = sb("ssb", [128, 2 * D])
    tbf = sb("tbf", [128, D], BF16)
    gsb = sb("gsb", [128, 2 * D])
    f1b = sb("f1b", [128, D])
    outt = sb("outt", [128, 2 * D])
    xqg = sb("xqg", [128, 2 * 1024], FP8)
    wq = sb("wq", [128, FCN * D], FP8)
    wtrot = sb("wtrot", [128, 2 * D])

    ps = [nc.alloc_psum_tensor(f"ps{i}", [128, 512], F32).ap() for i in range(8)]

    d_xs = [Ctr(nc, f"d_xs{i}") for i in range(12)]
    d_wc = Ctr(nc, "d_wc")
    d_sm = Ctr(nc, "d_sm")
    d_wt = [Ctr(nc, "d_wt0"), Ctr(nc, "d_wt1")]
    d_xg = [Ctr(nc, f"d_xg{i}") for i in range(4)]
    d_out = [Ctr(nc, "d_out0"), Ctr(nc, "d_out1")]
    d_dbg = Ctr(nc, "d_dbg")
    c_pe = Ctr(nc, "c_pe")
    c_v = Ctr(nc, "c_v")
    c_s = Ctr(nc, "c_s")
    c_g = Ctr(nc, "c_g")
    st_done_v = Ctr(nc, "st_done_v")
    st_done_s = Ctr(nc, "st_done_s")
    barc = Ctr(nc, "barc")

    bank_tok = [[] for _ in range(8)]

    def pe_use_bank(i):
        for (ctr, val) in bank_tok[i]:
            PE.wait_ge(ctr.h, val)
        bank_tok[i] = []

    def bank_freed(i, ctr):
        bank_tok[i].append((ctr, ctr.n))

    def bcast(src_1xN, dst_128xN, n, bank=0, v_tok=None, s_tok=None):
        PE.wait_ge(c_v.h, c_v.n if v_tok is None else v_tok)
        PE.wait_ge(c_s.h, c_s.n if s_tok is None else s_tok)
        off = 0
        while off < n:
            w = min(512, n - off)
            pe_use_bank(bank)
            pev = c_pe.inc(PE.matmul(ps[bank][:, 0:w], ones_row[0:1, :],
                                     src_1xN[0:1, off:off + w],
                                     start=True, stop=True))
            c_pe.wait(V, pev)
            c_v.inc(V.tensor_copy(dst_128xN[:, off:off + w], ps[bank][:, 0:w]))
            bank_freed(bank, c_v)
            off += w
        return c_v.n

    def pmax_col(src_col, dst_col, bank=1, v_tok=None, s_tok=None):
        pe_use_bank(bank)
        PE.wait_ge(c_v.h, c_v.n if v_tok is None else v_tok)
        if s_tok is not None:
            PE.wait_ge(c_s.h, s_tok)
        ptv = c_pe.inc(PE.matmul(ps[bank][0:1, 0:128], src_col[:, 0:1],
                                 ident[:, :], start=True, stop=True,
                                 is_transpose=True))
        c_pe.wait(V, ptv)
        tv = c_v.inc(V.reduce_max(tmp11[0:1, 0:1], ps[bank][0:1, 0:128], AX.X))
        bank_freed(bank, c_v)
        return bcast(tmp11, dst_col, 1, bank=bank, v_tok=tv, s_tok=0)

    def barrier():
        for e in ENGS:
            barc.inc(e.nop())
        n = barc.n
        for e in ENGS:
            e.wait_ge(barc.h, n)

    def dbg_write(name, src_ap):
        if name in dbg_d:
            c_v.wait(G)
            c_s.wait(G)
            G.wait_ge(d_dbg.h, d_dbg.n)
            d_dbg.inc(G.dma_start(dbg_d[name][:, :], src_ap), 16)

    def dbg_flush():
        if d_dbg.n:
            V.wait_ge(d_dbg.h, d_dbg.n)
            S.wait_ge(d_dbg.h, d_dbg.n)

    d_sm.inc(G.dma_start(dlg_sb[:, :], dlg_d[:, :]), 16)
    d_sm.inc(G.dma_start(fb_sb[:, :], fb_d[:, :]), 16)
    d_sm.inc(G.dma_start(osc_sb[:, :], osc_d[:, :]), 16)
    d_sm.inc(G.dma_start(zc_sb[:, :], zc_d[:, :]), 16)
    small_loaded = d_sm.n
    d_wc.inc(S.dma_start(wall_sb[:, :], wall_d[:, :]), 16)
    w_loaded = d_wc.n
    d_xg[0].inc(S.dma_start(xgrp[:, 0:1024], xga_d[:, 0:1024]), 16)
    pre0 = (0, d_xg[0].n)
    d_xg[1].inc(S.dma_start(xgrp[:, 1024:2048], xga_d[:, 1024:2048]), 16)
    pre1 = (1, d_xg[1].n)

    c_g.inc(G.iota(iof[:, :], [[1, 128]], base=0, channel_multiplier=0,
                   allow_small_or_imprecise_dtypes=True))
    c_g.inc(G.iota(iop[:, :], [[0, 128]], base=0, channel_multiplier=1,
                   allow_small_or_imprecise_dtypes=True))
    giota = c_g.inc(G.iota(ioc[:, :], [[0, 1]], base=0, channel_multiplier=1,
                           allow_small_or_imprecise_dtypes=True))
    c_g.wait(V, giota)
    c_v.inc(V.memset(ones_row[:, :], 1.0))
    c_v.inc(V.memset(one_col[:, :], 1.0))
    c_v.inc(V.memset(tmp13[:, :], 0.0))
    fence_init = c_v.inc(V.memset(tmp12[:, :], 0.0))
    c_v.inc(V.memset(negd[:, :], -1.0 / D))
    c_v.inc(V.tensor_tensor(ident[:, :], iop[:, :], iof[:, :], ALU.is_equal))
    c_v.inc(V.tensor_tensor(tri[:, :], iop[:, :], iof[:, :], ALU.is_gt))


    rrctr = [0]

    def rr_bank():
        i = rrctr[0] % 8
        rrctr[0] += 1
        return i

    sup_tok = [dict() for _ in range(NSUP)]
    s_fence_tok = [0]

    def s_fence():
        if s_fence_tok[0] == 0:
            S.wait_ge(c_v.h, fence_init)
        else:
            S.wait_ge(c_s.h, s_fence_tok[0])
        s_fence_tok[0] = c_s.inc(S.activation(tmp13[0:1, 0:1], tmp13[0:1, 0:1],
                                              AF.Identity))
        return s_fence_tok[0]

    def sup_tile(s, k):
        slot = s % 6
        return xsup[:, (slot * SUP + k) * 1024:(slot * SUP + k + 1) * 1024]

    def st_load(s):
        slot = s % 6
        if s >= 6:
            SY.wait_ge(st_done_v.h, s - 5)
            SY.wait_ge(st_done_s.h, s - 5)
        dst = xsup[:, slot * SUP * 1024:(slot + 1) * SUP * 1024]
        d_xs[slot].inc(SY.dma_start(
            dst, xs_d[:, s * SUP * 1024:(s + 1) * SUP * 1024]), 16)
        sup_tok[s]["dma"] = d_xs[slot].n

    def st_sum(s):
        slot = s % 6
        S.wait_ge(d_xs[slot].h, sup_tok[s]["dma"])
        c_s.inc(S.activation(sup_tile(s, 0), sup_tile(s, 0), AF.Identity,
                             accum_out=SUMW[:, s * SUP: s * SUP + 1]))
        if s % 4 != 3:
            V.wait_ge(d_xs[slot].h, sup_tok[s]["dma"])
            c_v.inc(V.reduce_sum(SUMW[:, s * SUP + 1: s * SUP + 2],
                                 sup_tile(s, 1), AX.X))
            sup_tok[s]["sum_v"] = c_v.n
        else:
            c_s.inc(S.activation(sup_tile(s, 1), sup_tile(s, 1), AF.Identity,
                                 accum_out=SUMW[:, s * SUP + 1: s * SUP + 2]))
            sup_tok[s]["sum_v"] = None
        sup_tok[s]["sum_s"] = s_fence()

    def st_negm(s):
        t0 = s * SUP
        G.wait_ge(c_s.h, sup_tok[s]["sum_s"])
        if sup_tok[s]["sum_v"] is not None:
            G.wait_ge(c_v.h, sup_tok[s]["sum_v"])
        c_g.inc(G.tensor_tensor(NEGMW[:, t0:t0 + SUP], SUMW[:, t0:t0 + SUP],
                                negd[:, :], ALU.mult))
        sup_tok[s]["negm"] = c_g.n

    def st_dev(s):
        S.wait_ge(c_g.h, sup_tok[s]["negm"])
        for k in range(SUP):
            t = s * SUP + k
            tile = sup_tile(s, k)
            c_s.inc(S.activation(tile, tile, AF.Square, bias=NEGMW[:, t:t + 1],
                                 accum_out=SQW[:, t:t + 1]))
        s_fence()
        S.wait_ge(st_done_s.h, st_done_s.n)
        S.wait_ge(c_v.h, fence_init)
        st_done_s.inc(S.activation(tmp12[0:1, 0:1], tmp12[0:1, 0:1],
                                   AF.Identity))

    def st_max(s):
        V.wait_ge(st_done_s.h, s + 1)
        for k in range(SUP):
            t = s * SUP + k
            ins = V.reduce_max(MAXDW[:, t:t + 1], sup_tile(s, k), AX.X)
            if k == SUP - 1:
                st_done_v.inc(ins)
            else:
                c_v.inc(ins)

    NSTEP = NSUP + 4
    step_ctr = [0]
    ld_ptr = [0]

    def load_next():
        s = ld_ptr[0]
        if s < NSUP:
            st_load(s)
            ld_ptr[0] += 1

    def sup_step():
        k = step_ctr[0]
        if k >= NSTEP:
            return
        step_ctr[0] += 1
        if k < NSUP:
            load_next()
        if 0 <= k - 1 < NSUP:
            st_sum(k - 1)
        if 0 <= k - 2 < NSUP:
            st_negm(k - 2)
        if 0 <= k - 3 < NSUP:
            st_dev(k - 3)
        if 0 <= k - 4 < NSUP:
            st_max(k - 4)

    chain_toks = {}

    def chain_rstd(c0, c1, sup_hi=None):
        V.wait_ge(st_done_s.h, st_done_s.n if sup_hi is None else sup_hi)
        if sup_hi is None:
            V.wait_ge(c_s.h, c_s.n)
            V.drain()
        vv = c_v.inc(V.tensor_scalar(SCR2[:, c0:c1], SQW[:, c0:c1], 1.0 / D,
                                     EPS, op0=ALU.mult, op1=ALU.add))
        S.wait_ge(c_v.h, vv)
        sq = c_s.inc(S.activation(SCR2[:, c0:c1], SCR2[:, c0:c1], AF.Sqrt))
        V.wait_ge(c_s.h, sq)
        rv = c_v.inc(V.reciprocal(RSW[:, c0:c1], SCR2[:, c0:c1]))
        gtk = 0
        if c0 < 2 * NG:
            h1 = min(c1, 2 * NG)
            G.wait_ge(c_v.h, rv)
            G.wait_ge(c_g.h, c_g.n)
            gtk = c_g.inc(G.tensor_tensor(NMRW[:, c0:h1], NEGMW[:, c0:h1],
                                          RSW[:, c0:h1], ALU.mult))
        for c in range(c0, min(c1, 2 * NG)):
            chain_toks[c] = (rv, gtk)
        return rv

    for _ in range(6):
        load_next()
    first_hi = min(NSUP, 2)
    n_up = min(NSTEP, 5)
    for _ in range(n_up):
        sup_step()
    chain_rstd(0, min(NG, SUP * first_hi), sup_hi=first_hi)
    chained = [min(NG, SUP * first_hi)]

    S.wait_ge(d_sm.h, small_loaded)
    G.wait_ge(d_sm.h, small_loaded)
    PE.wait_ge(d_sm.h, small_loaded)
    se = c_s.inc(S.activation(sc1[:, :], dlg_sb[:, :], AF.Exp, scale=-1.0))
    V.wait_ge(c_s.h, se)
    a1 = c_v.inc(V.tensor_scalar_add(sc1[:, :], sc1[:, :], 1.0))
    V.wait_ge(c_v.h, a1)
    vd = c_v.inc(V.reciprocal(sc2[:, :], sc1[:, :]))
    S.wait_ge(c_v.h, vd)
    sl = c_s.inc(S.activation(sc1[:, :], sc2[:, :], AF.Ln))
    bcast(sc1, lnd_col, 1, bank=0, v_tok=vd, s_tok=sl)
    bcast(osc_sb, osc_col, 1, bank=1, v_tok=0, s_tok=0)
    blc = bcast(zc_sb, zc_col, 1, bank=2, v_tok=0, s_tok=0)
    V.wait_ge(c_v.h, blc)
    c_v.inc(V.tensor_scalar_mul(nlnd_col[:, :], lnd_col[:, :], -1.0))
    c_v.inc(V.tensor_scalar_mul(l127_col[:, :], lnd_col[:, :], float(L - 1)))
    vdl = c_v.inc(V.tensor_scalar_mul(sc2[:, :], sc1[:, :], float(L)))
    S.wait_ge(c_v.h, vdl)
    for rep in range(2):
        c_s.inc(S.activation(qk[:, rep * 128:rep * 128 + 128], iof[:, :],
                             AF.Exp, scale=nlnd_col[:, :], bias=nlnd_col[:, :]))
        c_s.inc(S.activation(qk[:, 256 + rep * 128:256 + rep * 128 + 128],
                             iof[:, :], AF.Exp, scale=nlnd_col[:, :],
                             bias=l127_col[:, :]))
        c_s.inc(S.activation(qk[:, 512 + rep * 128:512 + rep * 128 + 128],
                             iof[:, :], AF.Exp, scale=lnd_col[:, :]))
    c_s.inc(S.activation(djc[:, :], ioc[:, :], AF.Exp, scale=lnd_col[:, :]))
    sdl = c_s.inc(S.activation(sc2[:, :], sc2[:, :], AF.Exp))
    bcast(sc2, dl_col, 1, bank=0, v_tok=vdl, s_tok=sdl)
    bcast(fb_sb, fbb_sb, D, bank=1, v_tok=0, s_tok=0)
    c_v.inc(V.memset(stA[:, :], 0.0))
    decay_s_tok = c_s.n
    decay_v_tok = c_v.n
    V.wait_ge(c_s.h, decay_s_tok)


    xnt_rot = Rot(2)
    xnp_rot = Rot(2)
    zsb_rot = Rot(2)
    qk_rot = Rot(2)
    vtm_rot = Rot(2)
    kdt_rot = Rot(2)
    dqkd_rot = Rot(2)
    wsb_rot = Rot(2)
    sdb_rot = Rot(2)
    xgrp_rot = Rot(4)
    outt_rot = Rot(2)
    xqg_rot = Rot(2)
    ssb_rot = Rot(2)
    gsb_rot = Rot(2)
    tbf_rot = Rot(1)
    f1b_rot = Rot(1)

    def xn_ln(statcol, xtile, sl, on_g):
        xt = xnt[:, sl * 1024:(sl + 1) * 1024]
        vt, gt = chain_toks[statcol]
        if on_g:
            xnt_rot.begin_write(G, sl)
            G.wait_ge(c_v.h, vt)
            bcc = [128, 1024]
            a1 = c_g.inc(G.tensor_tensor(
                xt, xtile, NEGMW[:, statcol:statcol + 1].to_broadcast(bcc),
                ALU.add))
            G.wait_ge(c_g.h, a1)
            return (c_g, c_g.inc(G.tensor_tensor(
                xt, xt, RSW[:, statcol:statcol + 1].to_broadcast(bcc),
                ALU.mult)))
        xnt_rot.begin_write(S, sl)
        S.wait_ge(c_v.h, vt)
        S.wait_ge(c_g.h, gt)
        sv = c_s.inc(S.activation(xt, xtile, AF.Identity,
                                  scale=RSW[:, statcol:statcol + 1],
                                  bias=NMRW[:, statcol:statcol + 1]))
        return (c_s, sv)

    def xn_fm(sl, sv):
        xt = xnt[:, sl * 1024:(sl + 1) * 1024]
        xnp_rot.begin_write(V, sl)
        PE.wait_ge(sv[0].h, sv[1])
        dst = xnp[:, sl * 1024:(sl + 1) * 1024]
        for half in range(2):
            bank = rr_bank()
            pe_use_bank(bank)
            for q in range(4):
                fc = half * 4 + q
                pev = c_pe.inc(PE.matmul(ps[bank][:, q * 128:(q + 1) * 128],
                                         xt[:, fc * 128:(fc + 1) * 128],
                                         ident[:, :], start=(q == 0),
                                         stop=(q == 3), is_transpose=True))
            xnt_rot.used_by(sl, c_pe)
            c_pe.wait(V, pev)
            c_v.inc(V.tensor_copy(dst[:, half * 512:(half + 1) * 512],
                                  ps[bank][:, :]))
            bank_freed(bank, c_v)
        return c_v.n

    def project_z(sl, v_tok):
        src = xnp[:, sl * 1024:(sl + 1) * 1024]
        bank = rr_bank()
        pe_use_bank(bank)
        PE.wait_ge(c_v.h, v_tok)
        for bc in range(2):
            for fc in range(FCN):
                pev = c_pe.inc(PE.matmul(
                    ps[bank][:, bc * 128:(bc + 1) * 128],
                    bas_sb[:, fc * NBS + bc * 128: fc * NBS + bc * 128 + 128],
                    src[:, fc * 128:(fc + 1) * 128],
                    start=(bc == 0 and fc == 0),
                    stop=(bc == 1 and fc == FCN - 1)))
        xnp_rot.used_by(sl, c_pe)
        zsb_rot.begin_write(V, sl)
        c_pe.wait(V, pev)
        c_v.inc(V.tensor_copy(zsb[:, sl * 256:(sl + 1) * 256],
                              ps[bank][:, 0:256]))
        bank_freed(bank, c_v)
        return c_v.n

    def project_tm(sl, coefT, dst_ap, v_tok, scale_col=None):
        zg = zsb[:, sl * 256:(sl + 1) * 256]
        bank = rr_bank()
        pe_use_bank(bank)
        PE.wait_ge(c_v.h, v_tok)
        for bc in range(2):
            pev = c_pe.inc(PE.matmul(ps[bank][:, 0:256],
                                     zg[:, bc * 128:(bc + 1) * 128],
                                     coefT[:, bc * CH:(bc + 1) * CH],
                                     start=(bc == 0), stop=(bc == 1)))
        zsb_rot.used_by(sl, c_pe)
        if scale_col is not None:
            V.wait_ge(c_pe.h, pev)
            c_v.inc(V.tensor_scalar(dst_ap, ps[bank][:, 0:256], scale_col,
                                    None, op0=ALU.mult))
            bank_freed(bank, c_v)
            return c_v.n
        V.wait_ge(c_pe.h, pev)
        c_v.inc(V.tensor_copy(dst_ap, ps[bank][:, 0:256]))
        bank_freed(bank, c_v)
        return c_v.n

    grp_xgrp = (xgrp, xgrp_rot, d_xg, 4, 1024)

    def emit_grp_dma(gb, src_fn, pos):
        buf, rot, sems, nslots, stride = gb
        src_d, p = src_fn(pos)
        slot = pos % nslots
        rot.begin_write(S, slot)
        sems[slot].inc(S.dma_start(
            buf[:, slot * stride: slot * stride + 1024],
            src_d[:, p * 1024:(p + 1) * 1024]), 16)
        return slot, sems[slot].n

    orderA = list(range(NG - 1, -1, -1))

    def run_pipe(src_fn, statbase, order, stages, pumps_per_iter, extra=None,
                 gb=None, emit_order=None, statmap=None, ln_g=True,
                 dmaq_init=None):
        n_st = len(stages)
        nsl = len(order)
        if gb is None:
            gb = grp_xgrp
        buf, rot, sems, nslots, stride = gb
        toks = [dict() for _ in range(nsl)]
        dmaq = dict(dmaq_init) if dmaq_init else {}
        for j in range(min(2, nsl)):
            if j not in dmaq:
                dmaq[j] = emit_grp_dma(gb, src_fn, j)

        def emit_ln(j):
            sl = j % 2
            slot, dmaval = dmaq[j]
            lne = G if ln_g else S
            lne.wait_ge(sems[slot].h, dmaval)
            xtile = buf[:, slot * stride: slot * stride + 1024]
            sv = xn_ln(statmap(j) if statmap else statbase + order[j],
                       xtile, sl, ln_g)
            rot.used_by(slot, c_g if ln_g else c_s)
            toks[j]["xtile"] = xtile
            toks[j]["xslot"] = slot
            toks[j]["xrot"] = rot
            toks[j]["sv"] = sv

        emit_ln(0)
        for i in range(nsl + n_st - 1):
            if i + 1 < nsl:
                emit_ln(i + 1)
            if i + 2 < nsl:
                dmaq[i + 2] = emit_grp_dma(gb, src_fn, i + 2)
            for k in (emit_order if emit_order is not None else range(n_st)):
                j = i - k
                if 0 <= j < nsl:
                    stages[k](j, order[j], j % 2, toks[j])
            np_i = (pumps_per_iter(i) if callable(pumps_per_iter)
                    else pumps_per_iter)
            for _ in range(np_i):
                sup_step()
            if extra is not None and i < nsl:
                extra(i)

    st_cur, st_nxt = stA, stB
    stv = [(c_v, c_v.n)]

    def s_tr(j, g, sl, tk):
        tk["vtok"] = xn_fm(sl, tk["sv"])

    def s_z(j, g, sl, tk):
        tk["ztok"] = project_z(sl, tk["vtok"])

    def a_kv(j, g, sl, tk):
        kdt_rot.begin_write(V, sl)
        tk["ktok"] = project_tm(sl, kT_sb, kdtm[:, sl * 256:(sl + 1) * 256],
                                tk["ztok"], scale_col=djc[:, :])
        vtm_rot.begin_write(V, sl)
        tk["vtok2"] = project_tm(sl, vT_sb, vtmg[:, sl * 256:(sl + 1) * 256],
                                 tk["ztok"])

    def a_state(j, g, sl, tk):
        nonlocal st_cur, st_nxt
        bank = rr_bank()
        pe_use_bank(bank)
        PE.wait_ge(c_v.h, max(tk["ktok"], tk["vtok2"]))
        kd_ap = kdtm[:, sl * 256:(sl + 1) * 256]
        vt_ap = vtmg[:, sl * 256:(sl + 1) * 256]
        c_pe.inc(PE.matmul(ps[bank][:, 0:256], kd_ap[:, 0:128], vt_ap,
                           start=True, stop=False))
        pev = c_pe.inc(PE.matmul(ps[bank][:, 256:512], kd_ap[:, 128:256], vt_ap,
                                 start=False, stop=True))
        kdt_rot.used_by(sl, c_pe)
        vtm_rot.used_by(sl, c_pe)
        V.wait_ge(c_pe.h, pev)
        V.wait_ge(stv[0][0].h, stv[0][1])
        stv[0] = (c_v, c_v.inc(V.scalar_tensor_tensor(
            st_nxt[:, :], st_cur[:, :], dl_col[:, :], ps[bank][:, :],
            ALU.mult, ALU.add)))
        bank_freed(bank, c_v)
        st_cur, st_nxt = st_nxt, st_cur

    wrot = Rot(2)
    w_issue = [0]
    wtok = {}
    wsc2_tok = [0]

    def w_dma_issue():
        i = w_issue[0]
        if i >= 2 * FCN:
            return
        w_issue[0] += 1
        ii = i % FCN
        slot = i % 2
        wrot.begin_write(S, slot)
        d_wt[slot].inc(S.dma_start(wtrot[:, slot * D:(slot + 1) * D],
                                   wpm_d[:, ii * 1024:(ii + 1) * 1024]), 16)
        wtok[i] = d_wt[slot].n

    def wsc_chain():
        V.drain()
        wmv = c_v.inc(V.reduce_max(wm_col[:, :], WM8[:, :], AX.X))
        pt = pmax_col(wm_col, wamax_col, bank=rr_bank(), v_tok=wmv)
        V.wait_ge(c_v.h, pt)
        gu = c_v.inc(V.tensor_scalar_max(wu_col[:, :], wamax_col[:, :], 1.0))
        S.wait_ge(c_v.h, gu)
        si = c_s.inc(S.activation(wsc2i_col[:, :], wu_col[:, :], AF.Identity,
                                  scale=1.0 / 224.0))
        c_s.inc(S.activation(wdq4_col[:, :], wu_col[:, :], AF.Identity,
                             scale=4.0 / FP8MAX))
        V.wait_ge(c_s.h, c_s.n)
        wsc2_tok[0] = c_v.inc(V.reciprocal(wsc2_col[:, :], wsc2i_col[:, :]))

    def w_extra(idx):
        if idx < FCN:
            slot = idx % 2
            V.wait_ge(d_wt[slot].h, wtok[idx])
            c_v.inc(V.reduce_max(WM8[:, idx:idx + 1],
                                 wtrot[:, slot * D:(slot + 1) * D],
                                 AX.X, apply_absolute_value=True))
            wrot.used_by(slot, c_v)
            w_dma_issue()
            return
        if idx == FCN:
            wsc_chain()
        i = idx - FCN
        if i >= FCN:
            return
        slot = idx % 2
        S.wait_ge(d_wt[slot].h, wtok[idx])
        S.wait_ge(c_v.h, wsc2_tok[0])
        c_s.inc(S.activation(wq[:, i * D:(i + 1) * D],
                             wtrot[:, slot * D:(slot + 1) * D],
                             AF.Identity, scale=wsc2_col[:, :]))
        wrot.used_by(slot, c_s)
        w_dma_issue()

    pumps = min(2, -(-(NSTEP - n_up) // (2 * NG))) if NSTEP > n_up else 0


    PE.wait_ge(d_wc.h, w_loaded)

    def b_qkv(j, g, sl, tk):
        ztok = tk["ztok"]
        zg = zsb[:, sl * 256:(sl + 1) * 256]
        bank = rr_bank()
        pe_use_bank(bank)
        PE.wait_ge(c_v.h, ztok)
        first = True
        for half, coefT in ((0, qT_sb), (1, kT_sb)):
            for ck in range(2):
                for bc in range(2):
                    pev = c_pe.inc(PE.matmul(
                        ps[bank][:, half * 256 + ck * 128:
                                 half * 256 + ck * 128 + 128],
                        coefT[:, bc * CH + ck * 128: bc * CH + ck * 128 + 128],
                        zg[:, bc * 128:(bc + 1) * 128],
                        start=first, stop=(half == 1 and ck == 1 and bc == 1)))
                    first = False
        zsb_rot.used_by(sl, c_pe)
        dqkd_rot.begin_write(V, sl)
        base = sl * 768
        V.wait_ge(c_pe.h, pev)
        c_v.inc(V.tensor_tensor(dqkd[:, base:base + 256], ps[bank][:, 0:256],
                                qk[:, 0:256], ALU.mult))
        c_v.inc(V.tensor_tensor(dqkd[:, base + 256:base + 512],
                                ps[bank][:, 0:256], qk[:, 256:512], ALU.mult))
        vkd = c_v.inc(V.tensor_tensor(dqkd[:, base + 512:base + 768],
                                      ps[bank][:, 256:512], qk[:, 512:768],
                                      ALU.mult))
        bank_freed(bank, c_v)
        vtm_rot.begin_write(V, sl)
        tk["vtok2"] = project_tm(sl, vT_sb, vtmg[:, sl * 256:(sl + 1) * 256],
                                 ztok)
        tk["vkd"] = vkd

    def b_chunk(j, g, sl, tk):
        nonlocal st_cur, st_nxt
        vkd, vtok2 = tk["vkd"], tk["vtok2"]
        base = sl * 768
        dq = dqkd[:, base: base + 256]
        dq2 = dqkd[:, base + 256: base + 512]
        kd = dqkd[:, base + 512: base + 768]
        vtm_ap = vtmg[:, sl * 256:(sl + 1) * 256]
        bank_w = rr_bank()
        pe_use_bank(bank_w)
        PE.wait_ge(c_v.h, vkd)
        c_pe.inc(PE.matmul(ps[bank_w][:, 0:128], kd[:, 0:128], dq[:, 0:128],
                           start=True, stop=False))
        pw = c_pe.inc(PE.matmul(ps[bank_w][:, 0:128], kd[:, 128:256],
                                dq[:, 128:256], start=False, stop=True))
        bank_kt = rr_bank()
        pe_use_bank(bank_kt)
        c_pe.inc(PE.matmul(ps[bank_kt][:, 0:128], kd[:, 0:128], ident[:, :],
                           start=True, stop=False, is_transpose=True))
        ptr = c_pe.inc(PE.matmul(ps[bank_kt][:, 128:256], kd[:, 128:256],
                                 ident[:, :], start=False, stop=True,
                                 is_transpose=True))
        wsb_rot.begin_write(V, sl)
        V.wait_ge(c_pe.h, pw)
        wv = c_v.inc(V.tensor_tensor(wsb[:, sl * 128:(sl + 1) * 128],
                                     ps[bank_w][:, 0:128], tri[:, :], ALU.mult))
        bank_freed(bank_w, c_v)
        kdt_rot.begin_write(S, sl)
        S.wait_ge(c_pe.h, ptr)
        kdv = c_s.inc(S.activation(kdtm[:, sl * 256:(sl + 1) * 256],
                                   ps[bank_kt][:, 0:256], AF.Identity))
        bank_freed(bank_kt, c_s)
        w_ap = wsb[:, sl * 128:(sl + 1) * 128]
        bank_r = rr_bank()
        pe_use_bank(bank_r)
        PE.wait_ge(c_v.h, max(wv, vtok2))
        PE.wait_ge(stv[0][0].h, stv[0][1])
        first = True
        for cv in range(2):
            col = cv * 128
            c_pe.inc(PE.matmul(ps[bank_r][:, col:col + 128],
                               vtm_ap[:, cv * 128:(cv + 1) * 128], w_ap,
                               start=first, stop=False))
            first = False
            c_pe.inc(PE.matmul(ps[bank_r][:, col:col + 128],
                               st_cur[:, cv * 128:cv * 128 + 128], dq2[:, 0:128],
                               start=False, stop=False))
            pr = c_pe.inc(PE.matmul(ps[bank_r][:, col:col + 128],
                                    st_cur[:, 256 + cv * 128:256 + cv * 128 + 128],
                                    dq2[:, 128:256], start=False,
                                    stop=(cv == 1)))
        bank_u = rr_bank()
        pe_use_bank(bank_u)
        PE.wait_ge(c_s.h, kdv)
        kd_tm = kdtm[:, sl * 256:(sl + 1) * 256]
        c_pe.inc(PE.matmul(ps[bank_u][:, 0:256], kd_tm[:, 0:128], vtm_ap,
                           start=True, stop=False))
        pst = c_pe.inc(PE.matmul(ps[bank_u][:, 256:512], kd_tm[:, 128:256],
                                 vtm_ap, start=False, stop=True))
        kdt_rot.used_by(sl, c_pe)
        vtm_rot.used_by(sl, c_pe)
        dqkd_rot.used_by(sl, c_pe)
        wsb_rot.used_by(sl, c_pe)
        S.wait_ge(c_pe.h, pr)
        c_s.inc(S.activation(RETR[:, g * 128:(g + 1) * 128],
                             ps[bank_r][:, 0:128], AF.Identity))
        tk["retr"] = c_s.inc(S.activation(RETR[:, TOK + g * 128:
                                               TOK + (g + 1) * 128],
                                          ps[bank_r][:, 128:256], AF.Identity))
        bank_freed(bank_r, c_s)
        V.wait_ge(c_pe.h, pst)
        V.wait_ge(stv[0][0].h, stv[0][1])
        stv[0] = (c_v, c_v.inc(V.scalar_tensor_tensor(
            st_nxt[:, :], st_cur[:, :], dl_col[:, :], ps[bank_u][:, :],
            ALU.mult, ALU.add)))
        bank_freed(bank_u, c_v)
        st_cur, st_nxt = st_nxt, st_cur

    def c_r2(j, g, sl, tk):
        bank = rr_bank()
        pe_use_bank(bank)
        PE.wait_ge(c_s.h, tk["retr"])
        first = True
        for bc2 in range(2):
            for cc in range(2):
                p2 = c_pe.inc(PE.matmul(
                    ps[bank][:, bc2 * 128:(bc2 + 1) * 128],
                    oc_sb[:, cc * NBS + bc2 * 128: cc * NBS + bc2 * 128 + 128],
                    RETR[:, cc * TOK + g * 128: cc * TOK + (g + 1) * 128],
                    start=first, stop=(bc2 == 1 and cc == 1)))
                first = False
        V.wait_ge(c_pe.h, p2)
        c_v.inc(V.tensor_copy(RETR[:, g * 128:(g + 1) * 128],
                              ps[bank][:, 0:128]))
        c_v.inc(V.tensor_copy(RETR[:, TOK + g * 128: TOK + (g + 1) * 128],
                              ps[bank][:, 128:256]))
        bank_freed(bank, c_v)

    def u_proj(j, g, sl, tk):
        (a_kv if j < NG else b_qkv)(j, g, sl, tk)

    def u_mix(j, g, sl, tk):
        if j < NG:
            a_state(j, g, sl, tk)
            return
        if j == NG:
            ctr0, v0 = stv[0]
            V.wait_ge(ctr0.h, v0)
            stv[0] = (c_v, c_v.inc(V.tensor_scalar(
                st_cur[:, :], st_cur[:, :], zc_col[:, :], None, op0=ALU.mult)))
            dbg_write("dbg_stateA", st_cur[:, :])
            dbg_flush()
        b_chunk(j, g, sl, tk)

    def u_r2(j, g, sl, tk):
        if j >= NG:
            c_r2(j, g, sl, tk)

    def ab_extra(i):
        if chained[0] < 2 * NG:
            c0 = chained[0]
            chain_rstd(c0, c0 + 2, sup_hi=min(NSUP, (c0 + 2) // 2))
            chained[0] += 2
        if i == NG - 2:
            w_dma_issue()
            w_dma_issue()
        if i >= NG:
            w_extra(i - NG)

    run_pipe(lambda pos: (xga_d, pos) if pos < NG else (xgb_d, pos - NG),
             0, orderA + orderA, [s_tr, s_z, u_proj, u_mix, u_r2],
             lambda i: 2 if i < 3 else (1 if i < 12 else 2),
             extra=ab_extra, statmap=lambda j: j,
             dmaq_init={0: pre0, 1: pre1}, emit_order=[2, 0, 1, 3, 4])
    for idx in range(NG, 2 * FCN):
        w_extra(idx)
    dbg_write("dbg_stateB", st_cur[:, :])
    dbg_flush()

    while step_ctr[0] < NSTEP:
        sup_step()
    rv_all = chain_rstd(0, NSTAT)
    S.wait_ge(c_v.h, rv_all)
    S.wait_ge(c_s.h, c_s.n)
    sq2 = c_s.inc(S.activation(SCR2[:, :], RSW[:, :], AF.Square))
    V.wait_ge(c_s.h, sq2)
    c_v.inc(V.tensor_mul(SCR1[:, :], MAXDW[:, :], SCR2[:, :]))
    vd = c_v.inc(V.drain())
    c_v.wait(V, vd)
    rv = c_v.inc(V.reduce_max(cand_col[:, :], SCR1[:, :], AX.X))
    pmax_col(cand_col, amax2_col, bank=rr_bank(), v_tok=rv)
    S.wait_ge(c_v.h, c_v.n)
    sqa = c_s.inc(S.activation(amax_col[:, :], amax2_col[:, :], AF.Sqrt))
    V.wait_ge(c_s.h, sqa)
    gu = c_v.inc(V.tensor_scalar_max(xu_col[:, :], amax_col[:, :], 1.0))
    S.wait_ge(c_v.h, gu)
    c_s.inc(S.activation(xsc2i_col[:, :], xu_col[:, :], AF.Identity,
                         scale=1.0 / 224.0))
    sxd = c_s.inc(S.activation(xdq_col[:, :], xu_col[:, :], AF.Identity,
                               scale=1.0 / FP8MAX))
    S.wait_ge(c_s.h, sxd)
    sxw_tok = c_s.inc(S.activation(sxw_col[:, :], xdq_col[:, :], AF.Identity,
                                   scale=wdq4_col[:, :]))
    V.wait_ge(c_s.h, sxw_tok)
    c_v.inc(V.reciprocal(xsc2_col[:, :], xsc2i_col[:, :]))
    dbg_write("dbg_amax", amax_col[:, :])
    dbg_write("dbg_rs", RSW[:, :])
    dbg_write("dbg_negm", NEGMW[:, :])
    dbg_write("dbg_wq", wq[:, 0:D])
    dbg_flush()

    barrier()

    def c_quant(j, g, sl, tk):
        xqg_rot.begin_write(V, sl)
        src = xnp[:, sl * 1024:(sl + 1) * 1024]
        V.wait_ge(c_v.h, tk["vtok"])
        tk["qv"] = c_v.inc(V.tensor_scalar(xqg[:, sl * 1024:(sl + 1) * 1024],
                                           src, xsc2_col[:, :], None,
                                           op0=ALU.mult))
        xnp_rot.used_by(sl, c_v)

    def c_sffn(j, g, sl, tk):
        qv = tk["qv"]
        ssb_rot.begin_write(S, sl)
        for fh in range(2):
            bank = rr_bank()
            pe_use_bank(bank)
            for bc in range(2):
                pp = c_pe.inc(PE.matmul(
                    ps[bank][:, 0:512],
                    RETR[:, bc * TOK + g * 128: bc * TOK + (g + 1) * 128],
                    basT_sb[:, bc * D + fh * 512: bc * D + fh * 512 + 512],
                    start=(bc == 0), stop=(bc == 1)))
            S.wait_ge(c_pe.h, pp)
            c_s.inc(S.activation(ssb[:, sl * D + fh * 512: sl * D + fh * 512 + 512],
                                 ps[bank][:, :], AF.Identity,
                                 scale=osc_col[:, :]))
            bank_freed(bank, c_s)
        xq_ap = xqg[:, sl * 1024:(sl + 1) * 1024]
        gsb_rot.begin_write(S, sl)
        PE.wait_ge(c_v.h, qv)
        for oh in range(2):
            bank = rr_bank()
            pe_use_bank(bank)
            for fc in range(FCN):
                pf = c_pe.inc(PE.matmul(
                    ps[bank][:, 0:512], xq_ap[:, fc * 128:(fc + 1) * 128],
                    wq[:, fc * D + oh * 512: fc * D + oh * 512 + 512],
                    start=(fc == 0), stop=(fc == FCN - 1)))
            xqg_rot.used_by(sl, c_pe)
            if oh == 0:
                tbf_rot.begin_write(V, 0)
            V.wait_ge(c_pe.h, pf)
            tv = c_v.inc(V.scalar_tensor_tensor(
                tbf[:, oh * 512:(oh + 1) * 512], ps[bank][:, :], sxw_col[:, :],
                fbb_sb[:, oh * 512:(oh + 1) * 512], ALU.mult, ALU.add))
            bank_freed(bank, c_v)
            S.wait_ge(c_v.h, tv)
            stok = c_s.inc(S.activation(gsb[:, sl * D + oh * 512:
                                            sl * D + oh * 512 + 512],
                                        tbf[:, oh * 512:(oh + 1) * 512],
                                        AF.Silu))
        tbf_rot.used_by(0, c_s)
        tk["gtok"] = stok

    def c_comb(j, g, sl, tk):
        f1b_rot.begin_write(G, 0)
        G.wait_ge(c_s.h, tk["gtok"])
        gmt = c_g.inc(G.tensor_mul(f1b[:, :], ssb[:, sl * D:(sl + 1) * D],
                                   gsb[:, sl * D:(sl + 1) * D]))
        ssb_rot.used_by(sl, c_g)
        gsb_rot.used_by(sl, c_g)
        oslot = sl
        slot = tk["xslot"]
        outt_rot.begin_write(V, oslot)
        V.wait_ge(c_g.h, gmt)
        ov = c_v.inc(V.tensor_add(outt[:, oslot * D:(oslot + 1) * D],
                                  f1b[:, :], tk["xtile"]))
        f1b_rot.used_by(0, c_v)
        tk["xrot"].used_by(slot, c_v)
        SY.wait_ge(c_v.h, ov)
        d_out[oslot].inc(SY.dma_start(out_d[:, g * 1024:(g + 1) * 1024],
                                      outt[:, oslot * D:(oslot + 1) * D]), 16)
        outt_rot.used_by(oslot, d_out[oslot])

    xsup_b2_rot = Rot(12)
    run_pipe(lambda pos: (xgb_d, pos), NG, orderA,
             [s_tr, c_quant, c_sffn, c_comb], 0,
             gb=(xsup, xsup_b2_rot, d_xs, 12, 1024), emit_order=[0, 2, 3, 1],
             statmap=lambda j: NG + j, ln_g=False)

    if d_dbg.n:
        G.wait_ge(d_dbg.h, d_dbg.n)
    SY.wait_ge(d_out[0].h, d_out[0].n)
    SY.wait_ge(d_out[1].h, d_out[1].n)
    return nc


def kernel(**inputs):
    xin = np.ascontiguousarray(np.asarray(inputs["L0_attn_x"], dtype=np.float32))
    x2d = xin.reshape(B * T, D)
    basis = np.ascontiguousarray(np.asarray(inputs["L0_attn_basis"], np.float32))
    qc = np.asarray(inputs["L0_attn_q_coeffs"], np.float32)
    kc = np.asarray(inputs["L0_attn_k_coeffs"], np.float32)
    vc = np.asarray(inputs["L0_attn_v_coeffs"], np.float32)
    ocf = np.ascontiguousarray(np.asarray(inputs["L0_attn_o_coeffs"], np.float32))
    dlg = np.asarray(inputs["L0_attn_decay_logit"], np.float32).reshape(1, 1)
    osc = np.asarray(inputs["L0_attn_out_scale"], np.float32).reshape(1, 1)
    w = np.asarray(inputs["L0_ffn_weight"], np.float32)
    fb = np.asarray(inputs["L0_ffn_bias"], np.float32).reshape(1, D)

    basT = np.ascontiguousarray(basis.T)
    qT = np.ascontiguousarray(qc.T)
    kT = np.ascontiguousarray(kc.T)
    vT = np.ascontiguousarray(vc.T)
    wT = np.ascontiguousarray(w.T)

    def pmaj(rows):
        n = rows.shape[0] // 128
        return np.ascontiguousarray(
            rows.reshape(n, 128, D).transpose(1, 0, 2).reshape(128, n * D))

    def pmaj_desc(rows):
        n = rows.shape[0] // 128
        return np.ascontiguousarray(
            rows.reshape(n, 128, D)[::-1].transpose(1, 0, 2).reshape(128, n * D))

    wpm = pmaj(wT)

    def blk(mat, n):
        r = np.asarray(mat, np.float32).reshape(n, 128, -1)
        return r.transpose(1, 0, 2).reshape(128, -1)

    wall = np.ascontiguousarray(np.concatenate(
        [blk(basis, 8), blk(basT, 2), blk(qT, 2), blk(kT, 2), blk(vT, 2),
         blk(ocf, 2)], axis=1))
    nc = build_nc()
    in_maps = []
    for core in range(8):
        b, h = divmod(core, 2)
        own0 = b * T + (2048 if h == 0 else 0)
        own = x2d[own0:own0 + TOK]
        others = np.concatenate([x2d[:own0], x2d[own0 + TOK:]], axis=0)
        pr0 = b * T if h == 1 else 0
        prev = others[pr0:pr0 + TOK]
        rest = np.concatenate([others[:pr0], others[pr0 + TOK:]], axis=0)
        prevd = prev.reshape(NG, 128, D)[::-1].reshape(TOK, D)
        ownd = own.reshape(NG, 128, D)[::-1].reshape(TOK, D)
        in_maps.append({
            "xs": pmaj(np.concatenate([prevd, ownd, rest], axis=0)),
            "xga": pmaj_desc(prev),
            "xgb": pmaj_desc(own),
            "wall": wall, "wpm": wpm, "fb": fb, "dlg": dlg, "osc": osc,
            "zc": np.array([[1.0 if h == 1 else 0.0]], np.float32),
        })
    res = run_bass_kernel_spmd(nc, in_maps, core_ids=list(range(8)))
    y = np.empty((B, T, D), np.float32)
    for core in range(8):
        b, h = divmod(core, 2)
        t0 = 2048 if h == 0 else 0
        opm = np.asarray(res.results[core]["out"])
        y[b, t0:t0 + TOK] = (opm.reshape(128, NG, D).transpose(1, 0, 2)
                             .reshape(TOK, D))
    return y
